# version 11
# baseline (speedup 1.0000x reference)
import contextlib

import numpy as np

import concourse.bass as bass
import concourse.mybir as mybir
from concourse.bass_utils import run_bass_kernel_spmd

F32 = mybir.dt.float32
BF16 = mybir.dt.bfloat16
Act = mybir.ActivationFunctionType
Alu = mybir.AluOpType
AxX = mybir.AxisListType.X

D = 1024
NST = 32
NE = 8
I2 = 2048
II = 1024
B = 4
T = 8192
TOK = 4096
TS = 512
NSUP = TOK // TS
DT = D // 128
TT = TS // 128
IIT = II // 128
PAIR = 8
LN_EPS = 1e-5
RMS_EPS = 2.220446049250313e-16
ALPHA = 1.702
LIMIT = 7.0
NEG_BIG = -3.0e38

DEBUG_TAPS = False

CO_ONES = 0
CO_DTB = 512
CO_BB = 1536
CO_CB = 1568
CO_GB = 1600
CO_B1 = 2048


def build_nc():
    nc = bass.Bass("TRN2", target_bir_lowering=False, debug=False,
                   detect_race_conditions=False)
    es = contextlib.ExitStack()

    def dr(name, shape, kind="ExternalInput"):
        return nc.dram_tensor(name, shape, F32, kind=kind).ap()

    dx = dr("x_own", [TOK, D])
    dxp = dr("x_pre", [TOK, D])
    d_dtw = dr("dtwT", [D, D])
    d_bw = dr("bwT", [D, NST])
    d_cw = dr("cwT", [D, NST])
    d_gw = dr("gwT", [D, NE])
    d_la = dr("logA", [D, NST])
    d_dtb = dr("dtb", [1, D])
    d_bb = dr("bb", [1, NST])
    d_cb = dr("cb", [1, NST])
    d_gb = dr("gb", [1, NE])
    d_dsk = dr("dsk", [D, 1])
    d_nw = dr("nw", [D, 1])
    d_w1 = dr("w1T", [NE, D, I2])
    d_b1 = dr("b1g", [NE, I2])
    d_w2 = dr("w2T", [NE, II, D])
    d_b2 = dr("b2", [NE, D])
    dy = dr("y", [TOK, D], kind="ExternalOutput")
    taps = {}
    if DEBUG_TAPS:
        for tn, shape in (("t_xT", [128, DT * TS]), ("t_lnT", [128, DT * TS]),
                          ("t_dtT", [128, DT * TS]), ("t_uT", [128, DT * TS]),
                          ("t_yscan", [128, DT * TS]), ("t_stat", [1, 2 * TS]),
                          ("t_glg", [128, TT * NE]), ("t_hst", [128, DT * NST]),
                          ("t_y2i", [128, DT * TS]), ("t_y2e0", [128, DT * TS]),
                          ("t_act", [128, IIT * TS]), ("t_y2f", [128, DT * TS]),
                          ("t_h2", [128, DT * TS])):
            taps[tn] = dr(tn, shape, kind="ExternalOutput")

    def sbt(name, shape, dt):
        return es.enter_context(nc.sbuf_tensor(name, shape, dt))

    x_tp = sbt("x_tp", [128, TT * D], F32)
    x_T = sbt("x_T", [128, DT * TS], F32)
    ln_T = sbt("ln_T", [128, DT * TS], BF16)
    ln2t = sbt("ln2t", [128, DT * TS], BF16)
    actt = sbt("actt", [128, IIT * TS], BF16)
    dt_T = sbt("dt_T", [128, DT * TS], F32)
    y_T = sbt("y_T", [128, DT * TS], F32)
    ub_s = sbt("ub_s", [128, DT * TS], F32)
    ab0 = sbt("ab0", [128, TS], F32)
    ab1 = sbt("ab1", [128, TS], F32)
    sg0 = sbt("sg0", [128, TS], F32)
    sg1 = sbt("sg1", [128, TS], F32)
    ring1 = sbt("ring1", [128, DT * I2], BF16)
    ring2 = sbt("ring2", [128, IIT * D], BF16)
    w_bw = sbt("w_bw", [128, DT * NST], BF16)
    w_cw = sbt("w_cw", [128, DT * NST], BF16)
    w_gw = sbt("w_gw", [128, DT * NE], BF16)
    a_s = sbt("a_s", [128, DT * NST], F32)
    na_s = sbt("na_s", [128, DT * NST], F32)
    hst = sbt("hst", [128, DT * NST], F32)
    hloc = sbt("hloc", [128, DT * NST], F32)
    dsk_s = sbt("dsk_s", [128, DT], F32)
    dtb_s = sbt("dtb_s", [128, DT], F32)
    nw_s = sbt("nw_s", [128, DT], F32)
    bpp = sbt("bpp", [32, TS], BF16)
    cpp = sbt("cpp", [32, TS], BF16)
    sel = sbt("sel", [32, NST * 128], BF16)
    glg = sbt("glg", [128, TT * NE], F32)
    gm = sbt("gm", [128, 3 * TT + 2 * TT * NE], F32)
    gT = sbt("gT", [8, TS], BF16)
    gTd = sbt("gTd", [8, TS], BF16)
    cneg = sbt("cneg", [128, TT * NE], F32)
    stat = sbt("stat", [1, 3 * TS], F32)
    crow = sbt("crow", [1, 2048], F32)
    crow_bf = sbt("crow_bf", [1, 4096], BF16)
    b2_s = sbt("b2_s", [8, D], BF16)
    onec = sbt("onec", [128, 1], F32)
    ident = sbt("ident", [128, 128], F32)
    brep_s = sbt("brep_s", [128, TS], F32)
    crep_s = sbt("crep_s", [128, TS], F32)
    ds_col = sbt("ds_col", [128, DT], F32)
    dec_s = sbt("dec_s", [128, NST], F32)
    bcol = sbt("bcol", [128, DT], F32)

    ps = [es.enter_context(nc.psum_tensor("psb%d" % i, [128, 512], F32))
          for i in range(8)]

    ln2_T = ln2t
    h2_T = ln_T
    actp = actt
    dtw_f = ring1[:, 0:DT * D]
    u_T = ln_T

    sems = {}
    for name in ("dx", "dw", "dg", "te", "v", "s", "g", "do"):
        sems[name] = es.enter_context(nc.semaphore(name))
    cnt = {k: 0 for k in sems}
    waited = {}

    def w(eng, ename, sem, val=None):
        v = cnt[sem] if val is None else val
        if v <= 0:
            return
        if waited.get((ename, sem), -1) >= v:
            return
        waited[(ename, sem)] = v
        eng.wait_ge(sems[sem], v)

    def inc(ins, sem, n=1):
        cnt[sem] += n
        ins.then_inc(sems[sem], n)
        return cnt[sem]

    dep = {}

    def vsync(e, ins):
        v_ = inc(ins, "v")
        w(e, "v", "v", v_)

    def need(eng, ename, res):
        if res in dep:
            s_, v_ = dep[res]
            w(eng, ename, s_, v_)

    def done(res, sem):
        dep[res] = (sem, cnt[sem])

    block = es.enter_context(nc.Block())

    bar = {k: 0 for k in cnt}

    def new_phase():
        for k in cnt:
            bar[k] = cnt[k]

    def _wrap(ename, f):
        def g(e):
            for k, v in bar.items():
                w(e, ename, k, v)
            f(e)
        return g

    def TE(f):
        block.tensor(_wrap("te", f))

    def V(f):
        block.vector(_wrap("v", f))

    def S(f):
        block.scalar(_wrap("s", f))

    def G(f):
        block.gpsimd(_wrap("g", f))

    def SY(f):
        block.sync(_wrap("sy", f))

    def su_sync(e):
        inc(e.dma_start(out=a_s[:].rearrange("p (k n) -> p k n", k=DT),
                        in_=d_la.rearrange("(k p) n -> p k n", p=128)), "dw", 16)
        inc(e.dma_start(out=dsk_s[:, :, None],
                        in_=d_dsk.rearrange("(k p) o -> p k o", p=128),
                        allow_slow_non_contiguous=True), "dw", 16)
        inc(e.dma_start(out=dtb_s[:, :, None],
                        in_=d_dtb.rearrange("o (k p) -> p k o", p=128),
                        allow_slow_non_contiguous=True), "dw", 16)
        inc(e.dma_start(out=nw_s[:, :, None],
                        in_=d_nw.rearrange("(k p) o -> p k o", p=128),
                        allow_slow_non_contiguous=True), "dw", 16)
    SY(su_sync)

    def su_sync1b(e):
        inc(e.dma_start(out=crow[0:1, CO_DTB:CO_DTB + D], in_=d_dtb), "dw", 16)
        inc(e.dma_start(out=crow[0:1, CO_BB:CO_BB + NST], in_=d_bb), "dw", 16)
        inc(e.dma_start(out=crow[0:1, CO_CB:CO_CB + NST], in_=d_cb), "dw", 16)
    SY(su_sync1b)


    def su_gp(e):
        inc(e.dma_start(out=w_bw[:].rearrange("p (k n) -> p k n", k=DT),
                        in_=d_bw.rearrange("(k p) n -> p k n", p=128)), "dg", 16)
        inc(e.dma_start(out=w_cw[:].rearrange("p (k n) -> p k n", k=DT),
                        in_=d_cw.rearrange("(k p) n -> p k n", p=128)), "dg", 16)
        inc(e.dma_start(out=w_gw[:].rearrange("p (k n) -> p k n", k=DT),
                        in_=d_gw.rearrange("(k p) n -> p k n", p=128)), "dg", 16)
        inc(e.dma_start(out=crow_bf[0:1, CO_DTB:CO_DTB + D], in_=d_dtb), "dg", 16)
        inc(e.dma_start(out=crow_bf[0:1, CO_BB:CO_BB + NST], in_=d_bb), "dg", 16)
        inc(e.dma_start(out=crow_bf[0:1, CO_CB:CO_CB + NST], in_=d_cb), "dg", 16)
        inc(e.dma_start(out=crow_bf[0:1, CO_GB:CO_GB + NE], in_=d_gb), "dg", 16)
        inc(e.dma_start(out=b2_s[:], in_=d_b2), "dg", 16)
        inc(e.memset(onec[:], 1.0), "g")
        inc(e.memset(crow[0:1, 0:512], 1.0), "g")
        inc(e.memset(crow_bf[0:1, CO_ONES:CO_ONES + 512], 1.0), "g")
        inc(e.memset(hst[:], 0.0), "g")
        inc(e.memset(cneg[:], NEG_BIG), "g")
    G(su_gp)
    g_ms = cnt["g"]

    def su_iota(e):
        w(e, "v", "g", g_ms)
        ins = e.tensor_tensor_scan(
            out=stat[0:1, 0:128],
            data0=onec[0:1, 0:1].broadcast_to([1, 128]),
            data1=onec[0:1, 0:1].broadcast_to([1, 128]),
            initial=-1.0, op0=Alu.mult, op1=Alu.add)
        inc(ins, "v")
    V(su_iota)
    v_iota = cnt["v"]

    def su_te(e):
        w(e, "te", "v", v_iota)
        e.matmul(ps[6][:, 0:1], stat[0:1, 0:128], onec[0:1, 0:1],
                 start=True, stop=True)
        inc(e.matmul(ps[7][:, 0:128], crow[0:1, 0:128], stat[0:1, 0:128],
                     start=True, stop=True), "te")
    TE(su_te)
    te_iota = cnt["te"]

    def su_dve0(e):
        w(e, "v", "te", te_iota)
        vsync(e, e.tensor_copy(out=bcol[:, 0:1], in_=ps[6][:, 0:1]))
        vsync(e, e.tensor_copy(out=x_tp[:, 0:128], in_=ps[7][:, 0:128]))
        vsync(e, e.tensor_tensor(out=ident[:],
                                 in0=bcol[:, 0:1].broadcast_to([128, 128]),
                                 in1=x_tp[:, 0:128], op=Alu.is_equal))
        for n in range(NST):
            ins = e.tensor_scalar(
                out=sel[0:32, n * 128:(n + 1) * 128],
                in0=bcol[0:32, 0:1].broadcast_to([32, 128]),
                scalar1=float(n), scalar2=None, op0=Alu.is_equal)
        inc(ins, "v")
        done("ps6", "v")
        done("ps7", "v")
    V(su_dve0)

    def su_sca(e):
        w(e, "s", "dw", cnt["dw"])
        inc(e.activation(out=na_s[:], in_=a_s[:], func=Act.Exp), "s")
    S(su_sca)

    def su_dve(e):
        w(e, "v", "s", cnt["s"])
        inc(e.tensor_scalar_mul(out=a_s[:], in0=na_s[:], scalar1=-1.0), "v")
    V(su_dve)
    new_phase()

    def emit_ln(src_T, dst_T):
        def f_te(e):
            need(e, "te", "ps6")
            for k in range(DT):
                ins = e.matmul(ps[6][0:1, 0:TS], onec[:],
                               src_T[:, k * TS:(k + 1) * TS],
                               start=(k == 0), stop=(k == DT - 1))
            inc(ins, "te")
        TE(f_te)

        te_sq0 = cnt["te"] + 1

        def f_sca(e):
            for k in range(DT):
                p = k % 2
                if k < 2:
                    need(e, "s", "sqfree%d" % p)
                else:
                    w(e, "s", "te", te_sq0 + (k - 2) + 1)
                ins = e.activation(out=(sg0 if p == 0 else sg1)[:],
                                   in_=src_T[:, k * TS:(k + 1) * TS],
                                   func=Act.Square)
                inc(ins, "s")
        S(f_sca)

        sq_first = cnt["s"] - DT

        def f_te2b(e):
            need(e, "te", "ps7")
            for k in range(DT):
                p = k % 2
                w(e, "te", "s", sq_first + k + 1)
                ins = e.matmul(ps[7][0:1, 0:TS], onec[:],
                               (sg0 if p == 0 else sg1)[:],
                               start=(k == 0), stop=(k == DT - 1))
                inc(ins, "te")
                done("sqfree%d" % p, "te")
        TE(f_te2b)
        te_stats = cnt["te"]

        def f_dve(e):
            w(e, "v", "te", te_stats)
            vsync(e, e.tensor_scalar_mul(out=stat[0:1, 0:TS],
                                         in0=ps[6][0:1, 0:TS], scalar1=1.0 / D))
            vsync(e, e.tensor_scalar_mul(out=stat[0:1, TS:2 * TS],
                                         in0=ps[7][0:1, 0:TS], scalar1=1.0 / D))
            vsync(e, e.tensor_tensor(out=stat[0:1, 2 * TS:3 * TS],
                                     in0=stat[0:1, 0:TS],
                                     in1=stat[0:1, 0:TS], op=Alu.mult))
            vsync(e, e.tensor_tensor(out=stat[0:1, TS:2 * TS],
                                     in0=stat[0:1, TS:2 * TS],
                                     in1=stat[0:1, 2 * TS:3 * TS],
                                     op=Alu.subtract))
            vsync(e, e.tensor_scalar_add(out=stat[0:1, TS:2 * TS],
                                         in0=stat[0:1, TS:2 * TS],
                                         scalar1=LN_EPS))
            ins = e.reciprocal(out=stat[0:1, TS:2 * TS],
                               in_=stat[0:1, TS:2 * TS])
            inc(ins, "v")
            done("ps6", "v")
            done("ps7", "v")
        V(f_dve)
        v_st = cnt["v"]

        def f_sca2(e):
            w(e, "s", "v", v_st)
            inc(e.activation(out=stat[0:1, TS:2 * TS],
                             in_=stat[0:1, TS:2 * TS], func=Act.Sqrt), "s")
        S(f_sca2)
        s_rstd = cnt["s"]

        def f_dve2(e):
            w(e, "v", "s", s_rstd)
            inc(e.tensor_tensor(out=stat[0:1, 0:TS], in0=stat[0:1, 0:TS],
                                in1=stat[0:1, TS:2 * TS], op=Alu.mult), "v")
        V(f_dve2)
        v_nu = cnt["v"]

        def f_te3(e):
            w(e, "te", "v", v_nu)
            need(e, "te", "ps2")
            need(e, "te", "ps3")
            e.matmul(ps[2][:, 0:TS], crow[0:1, CO_ONES:CO_ONES + 128],
                     stat[0:1, TS:2 * TS], start=True, stop=True)
            inc(e.matmul(ps[3][:, 0:TS], crow[0:1, CO_ONES:CO_ONES + 128],
                         stat[0:1, 0:TS], start=True, stop=True), "te")
        TE(f_te3)
        te_rep = cnt["te"]

        def f_dve3(e):
            w(e, "v", "te", te_rep)
            for k in range(DT):
                e.tensor_tensor(out=sg0[:], in0=src_T[:, k * TS:(k + 1) * TS],
                                in1=ps[2][:, 0:TS], op=Alu.mult)
                ins = e.tensor_tensor(out=dst_T[:, k * TS:(k + 1) * TS],
                                      in0=sg0[:], in1=ps[3][:, 0:TS],
                                      op=Alu.subtract)
            inc(ins, "v")
            done("ps2", "v")
            done("ps3", "v")
            done("sqfree0", "v")
            done("sqfree1", "v")
        V(f_dve3)
        new_phase()

    dtw_dg = [0]
    def tap(name, src_ap):
        if not DEBUG_TAPS or name not in taps:
            return
        new_phase()
        def f_g(e):
            ins = e.dma_start(out=taps[name], in_=src_ap)
            inc(ins, "dg", 16)
        G(f_g)
        new_phase()

    def emit_load_transpose(sup, prefix):
        xsrc = dxp if prefix else dx
        t0 = sup * TS

        def f_gdtw(e):
            need(e, "g", "ring1")
            inc(e.dma_start(out=dtw_f.rearrange("p (k c) -> p k c", k=DT),
                            in_=d_dtw.rearrange("(k p) c -> p k c", p=128)),
                "dg", 16)
        G(f_gdtw)
        dtw_dg[0] = cnt["dg"]

        def f_sy(e):
            ins = e.dma_start(
                out=x_tp[:].rearrange("p (i d) -> p i d", i=TT),
                in_=xsrc[t0:t0 + TS, :].rearrange("(i p) d -> p i d", p=128))
            inc(ins, "dx", 16)
        SY(f_sy)
        dxv = cnt["dx"]

        tx_prod = []
        v_evac0 = cnt["v"]

        def f_te(e):
            w(e, "te", "dx", dxv)
            for k in range(DT):
                bank = 2 + (k % 2)
                if k < 2:
                    need(e, "te", "ps%d" % bank)
                else:
                    w(e, "te", "v", v_evac0 + (k - 2) + 1)
                for i in range(TT):
                    ins = e.transpose(
                        ps[bank][:, i * 128:(i + 1) * 128],
                        x_tp[:, i * D + k * 128:i * D + (k + 1) * 128],
                        ident[:])
                tx_prod.append(inc(ins, "te"))
        TE(f_te)

        def f_dve(e):
            for k in range(DT):
                bank = 2 + (k % 2)
                w(e, "v", "te", tx_prod[k])
                ins = e.tensor_copy(out=x_T[:, k * TS:(k + 1) * TS],
                                    in_=ps[bank][:, 0:TS])
                inc(ins, "v")
                done("ps%d" % bank, "v")
        V(f_dve)
        new_phase()

    def emit_dt_u(sup, prefix):
        dtp_prod = []
        s_dt0 = cnt["s"]

        def f_te(e):
            w(e, "te", "dg", dtw_dg[0])
            for m in range(DT):
                bank = m % 2
                if m < 2:
                    need(e, "te", "ps%d" % bank)
                else:
                    w(e, "te", "s", s_dt0 + (m - 2) + 1)
                for k in range(DT):
                    ins = e.matmul(ps[bank][:, 0:TS],
                                   dtw_f[:, k * D + m * 128:k * D + (m + 1) * 128],
                                   ln_T[:, k * TS:(k + 1) * TS],
                                   start=(k == 0), stop=(k == DT - 1))
                dtp_prod.append(inc(ins, "te"))
            done("ring1", "te")
        TE(f_te)

        sp_prod = []

        def f_sca(e):
            for m in range(DT):
                bank = m % 2
                w(e, "s", "te", dtp_prod[m])
                e.activation(out=sg1[:], in_=ps[bank][:, 0:TS], func=Act.Exp,
                             bias=dtb_s[:, m:m + 1])
                ins = e.activation(out=dt_T[:, m * TS:(m + 1) * TS], in_=sg1[:],
                                   func=Act.Ln, bias=1.0)
                sp_prod.append(inc(ins, "s"))
                done("ps%d" % bank, "s")
        S(f_sca)

        new_phase()

    def emit_bc_proj(sup, prefix):
        def f_te(e):
            need(e, "te", "ps0")
            need(e, "te", "ps1")
            for k in range(DT):
                e.matmul(ps[0][0:NST, 0:TS],
                         w_bw[:, k * NST:(k + 1) * NST],
                         ln_T[:, k * TS:(k + 1) * TS],
                         start=(k == 0), stop=False)
            ins = e.matmul(ps[0][0:NST, 0:TS], crow_bf[0:1, CO_BB:CO_BB + NST],
                           crow_bf[0:1, CO_ONES:CO_ONES + TS],
                           start=False, stop=True)
            if not prefix:
                for k in range(DT):
                    e.matmul(ps[1][0:NST, 0:TS],
                             w_cw[:, k * NST:(k + 1) * NST],
                             ln_T[:, k * TS:(k + 1) * TS],
                             start=(k == 0), stop=False)
                ins = e.matmul(ps[1][0:NST, 0:TS], crow_bf[0:1, CO_CB:CO_CB + NST],
                               crow_bf[0:1, CO_ONES:CO_ONES + TS],
                               start=False, stop=True)
            inc(ins, "te")
        TE(f_te)
        te_bc = cnt["te"]

        def f_dve(e):
            w(e, "v", "te", te_bc)
            ins = e.tensor_copy(out=bpp[:], in_=ps[0][0:NST, 0:TS])
            if not prefix:
                ins = e.tensor_copy(out=cpp[:], in_=ps[1][0:NST, 0:TS])
            inc(ins, "v")
            done("ps0", "v")
            done("ps1", "v")
        V(f_dve)
        new_phase()

    def emit_uy(sup, prefix):
        def f_dve(e):
            for m in range(DT):
                if not prefix:
                    e.tensor_scalar(out=y_T[:, m * TS:(m + 1) * TS],
                                    in0=ln_T[:, m * TS:(m + 1) * TS],
                                    scalar1=dsk_s[:, m:m + 1],
                                    scalar2=None, op0=Alu.mult)
                ins = e.tensor_tensor(out=ln_T[:, m * TS:(m + 1) * TS],
                                      in0=dt_T[:, m * TS:(m + 1) * TS],
                                      in1=ln_T[:, m * TS:(m + 1) * TS],
                                      op=Alu.mult)
            inc(ins, "v")
        V(f_dve)
        new_phase()

    def emit_scan_main(sup):
        for n in range(NST):
            pb = 4 if n % 2 == 0 else 2
            pc = 5 if n % 2 == 0 else 3
            te_rep = [0]

            def f_te(e, n=n, pb=pb, pc=pc, te_rep=te_rep):
                need(e, "te", "ps%d" % pb)
                need(e, "te", "ps%d" % pc)
                e.matmul(ps[pb][:, 0:TS], sel[:, n * 128:(n + 1) * 128],
                         bpp[:], start=True, stop=True)
                te_rep[0] = inc(
                    e.matmul(ps[pc][:, 0:TS], sel[:, n * 128:(n + 1) * 128],
                             cpp[:], start=True, stop=True), "te")
            TE(f_te)

            v0 = cnt["v"]
            ab_prod = []

            def f_sca(e, n=n, v0=v0, ab_prod=ab_prod):
                for dblk in range(DT):
                    p = dblk % 2
                    ab = ab0 if p == 0 else ab1
                    w(e, "s", "v", v0 if dblk < 2 else v0 + dblk - 1)
                    ins = e.activation(
                        out=ab[:], in_=dt_T[:, dblk * TS:(dblk + 1) * TS],
                        func=Act.Exp,
                        scale=a_s[:, dblk * NST + n:dblk * NST + n + 1])
                    ab_prod.append(inc(ins, "s"))
            S(f_sca)

            sc_ev = [0]

            def f_scev(e, pc=pc, te_rep=te_rep):
                w(e, "s", "te", te_rep[0])
                need(e, "s", "crep")
                sc_ev[0] = inc(e.activation(out=crep_s[:], in_=ps[pc][:, 0:TS],
                                            func=Act.Copy), "s")
                done("ps%d" % pc, "s")
            S(f_scev)

            scan_inc = []
            hc = [0]

            def f_dve(e, n=n, pb=pb, te_rep=te_rep, ab_prod=ab_prod):
                w(e, "v", "te", te_rep[0])
                need(e, "v", "ubw")
                for dblk in range(DT):
                    e.tensor_tensor(
                        out=ub_s[:, dblk * TS:(dblk + 1) * TS],
                        in0=u_T[:, dblk * TS:(dblk + 1) * TS],
                        in1=ps[pb][:, 0:TS], op=Alu.mult)
                sv = 0
                for dblk in range(DT):
                    p = dblk % 2
                    ab = ab0 if p == 0 else ab1
                    w(e, "v", "s", ab_prod[dblk])
                    if dblk < DT // 2:
                        need(e, "v", "qx%d" % dblk)
                    col = dblk * NST + n
                    sv = inc(e.tensor_tensor_scan(
                        out=x_tp[:, dblk * TS:(dblk + 1) * TS],
                        data0=ab[:], data1=ub_s[:, dblk * TS:(dblk + 1) * TS],
                        initial=hst[:, col:col + 1],
                        op0=Alu.mult, op1=Alu.add), "v")
                    scan_inc.append(sv)
                w(e, "v", "v", sv)
                ins = e.tensor_copy(
                    out=hst[:, n::NST][:, 0:DT],
                    in_=x_tp[:, TS - 1::TS][:, 0:DT])
                hc[0] = inc(ins, "v")
                done("ps%d" % pb, "v")
                done("ubw", "v")
            V(f_dve)

            q_inc = []

            def f_gq(e, n=n, sc_ev=sc_ev, hc=hc):
                w(e, "g", "s", sc_ev[0])
                w(e, "g", "v", hc[0])
                for dblk in range(DT // 2):
                    ins = e.tensor_tensor(
                        out=x_tp[:, dblk * TS:(dblk + 1) * TS],
                        in0=x_tp[:, dblk * TS:(dblk + 1) * TS],
                        in1=crep_s[:], op=Alu.mult)
                    q_inc.append(inc(ins, "g"))
                    done("qx%d" % dblk, "g")
                done("crep", "g")
            G(f_gq)

            def f_dve2(e, n=n, q_inc=q_inc, sc_ev=sc_ev):
                w(e, "v", "s", sc_ev[0])
                for dblk in range(DT // 2, DT):
                    e.tensor_tensor(
                        out=x_tp[:, dblk * TS:(dblk + 1) * TS],
                        in0=x_tp[:, dblk * TS:(dblk + 1) * TS],
                        in1=crep_s[:], op=Alu.mult)
                w(e, "v", "g", q_inc[-1])
                ins = e.tensor_tensor(out=y_T[:], in0=y_T[:], in1=x_tp[:],
                                      op=Alu.add)
                inc(ins, "v")
                done("crep", "v")
            V(f_dve2)
        new_phase()

    def emit_scan_prefix(sup):
        def f_dve(e):
            for dblk in range(DT):
                sv = inc(e.tensor_tensor_scan(
                    out=y_T[:, dblk * TS:(dblk + 1) * TS],
                    data0=onec[:, 0:1].broadcast_to([128, TS]),
                    data1=dt_T[:, dblk * TS:(dblk + 1) * TS],
                    initial=0.0, op0=Alu.mult, op1=Alu.add), "v")
                w(e, "v", "v", sv)
                ins = e.tensor_copy(
                    out=ds_col[:, dblk:dblk + 1],
                    in_=y_T[:, (dblk + 1) * TS - 1:(dblk + 1) * TS])
            inc(ins, "v")
        V(f_dve)
        new_phase()

        prev_f_end = [0]
        for n in range(NST):
            pb = 4 if n % 2 == 0 else 2
            te_rep = [0]

            def f_te(e, n=n, pb=pb, te_rep=te_rep):
                need(e, "te", "ps%d" % pb)
                te_rep[0] = inc(
                    e.matmul(ps[pb][:, 0:TS], sel[:, n * 128:(n + 1) * 128],
                             bpp[:], start=True, stop=True), "te")
            TE(f_te)

            bc_prod = []
            sb_ev = [0]

            def f_scev2(e, pb=pb, te_rep=te_rep):
                w(e, "s", "te", te_rep[0])
                need(e, "s", "brep")
                sb_ev[0] = inc(e.activation(out=brep_s[:], in_=ps[pb][:, 0:TS],
                                            func=Act.Copy), "s")
                done("ps%d" % pb, "s")
            S(f_scev2)

            gub = [0]

            def f_gub(e, n=n, sb_ev=sb_ev, gub=gub):
                w(e, "g", "s", sb_ev[0])
                need(e, "g", "pubw")
                for dblk in range(DT):
                    ins = e.tensor_tensor(
                        out=ub_s[:, dblk * TS:(dblk + 1) * TS],
                        in0=u_T[:, dblk * TS:(dblk + 1) * TS],
                        in1=brep_s[:], op=Alu.mult)
                gub[0] = inc(ins, "g")
                done("brep", "g")
            G(f_gub)

            def f_dve2(e, n=n, bc_prod=bc_prod):
                w(e, "v", "s", prev_f_end[0])
                for dblk in range(DT):
                    ins = e.tensor_scalar(
                        out=bcol[:, dblk:dblk + 1],
                        in0=a_s[:, dblk * NST + n:dblk * NST + n + 1],
                        scalar1=ds_col[:, dblk:dblk + 1], scalar2=None,
                        op0=Alu.mult)
                    bc_prod.append(inc(ins, "v"))
            V(f_dve2)

            v1 = cnt["v"]
            f_prod = []

            def f_sca(e, n=n, bc_prod=bc_prod, f_prod=f_prod, v1=v1):
                for dblk in range(DT):
                    p = dblk % 2
                    ab = ab0 if p == 0 else ab1
                    w(e, "s", "v", bc_prod[dblk])
                    if dblk >= 2:
                        w(e, "s", "v", v1 + dblk - 1)
                    ins = e.activation(
                        out=ab[:], in_=y_T[:, dblk * TS:(dblk + 1) * TS],
                        func=Act.Exp,
                        scale=na_s[:, dblk * NST + n:dblk * NST + n + 1],
                        bias=bcol[:, dblk:dblk + 1])
                    f_prod.append(inc(ins, "s"))
            S(f_sca)
            prev_f_end[0] = cnt["s"]

            def f_dve3(e, n=n, f_prod=f_prod, gub=gub):
                w(e, "v", "g", gub[0])
                for dblk in range(DT):
                    p = dblk % 2
                    ab = ab0 if p == 0 else ab1
                    w(e, "v", "s", f_prod[dblk])
                    col = dblk * NST + n
                    ins = e.scalar_tensor_tensor(
                        out=sg0[:], in0=ab[:], scalar=1.0,
                        in1=ub_s[:, dblk * TS:(dblk + 1) * TS],
                        op0=Alu.mult, op1=Alu.mult,
                        accum_out=hloc[:, col:col + 1])
                    inc(ins, "v")
                done("pubw", "v")
            V(f_dve3)
        new_phase()

        for dblk in range(DT):
            vd = [0]

            def f_dve4(e, dblk=dblk, vd=vd):
                ins = e.tensor_scalar(
                    out=dec_s[:], in0=a_s[:, dblk * NST:(dblk + 1) * NST],
                    scalar1=ds_col[:, dblk:dblk + 1], scalar2=None,
                    op0=Alu.mult)
                vd[0] = inc(ins, "v")
            V(f_dve4)
            sd = [0]

            def f_sca2(e, vd=vd, sd=sd):
                w(e, "s", "v", vd[0])
                sd[0] = inc(e.activation(out=dec_s[:], in_=dec_s[:],
                                         func=Act.Exp), "s")
            S(f_sca2)

            def f_dve5(e, dblk=dblk, sd=sd):
                w(e, "v", "s", sd[0])
                vsync(e, e.tensor_tensor(
                    out=hst[:, dblk * NST:(dblk + 1) * NST],
                    in0=hst[:, dblk * NST:(dblk + 1) * NST],
                    in1=dec_s[:], op=Alu.mult))
                ins = e.tensor_tensor(out=hst[:, dblk * NST:(dblk + 1) * NST],
                                      in0=hst[:, dblk * NST:(dblk + 1) * NST],
                                      in1=hloc[:, dblk * NST:(dblk + 1) * NST],
                                      op=Alu.add)
                inc(ins, "v")
            V(f_dve5)
        new_phase()

    def emit_post_ssm(sup):
        t0 = sup * TS
        y2 = dt_T

        def f_dve(e):
            for k in range(DT):
                ins = e.tensor_tensor(out=y_T[:, k * TS:(k + 1) * TS],
                                      in0=y_T[:, k * TS:(k + 1) * TS],
                                      in1=x_T[:, k * TS:(k + 1) * TS],
                                      op=Alu.add)
            inc(ins, "v")
        V(f_dve)
        new_phase()

        emit_ln(y_T, ln2_T)

        sq_prod2 = []
        te_rms0 = cnt["te"]

        def f_sca(e):
            for k in range(DT):
                p = k % 2
                if k < 2:
                    need(e, "s", "sqfree%d" % p)
                else:
                    w(e, "s", "te", te_rms0 + (k - 2) + 1)
                ins = e.activation(out=(sg0 if p == 0 else sg1)[:],
                                   in_=ln2_T[:, k * TS:(k + 1) * TS],
                                   func=Act.Square)
                sq_prod2.append(inc(ins, "s"))
        S(f_sca)

        def f_te(e):
            need(e, "te", "ps7")
            for k in range(DT):
                p = k % 2
                w(e, "te", "s", sq_prod2[k])
                ins = e.matmul(ps[7][0:1, 0:TS], onec[:],
                               (sg0 if p == 0 else sg1)[:],
                               start=(k == 0), stop=(k == DT - 1))
                inc(ins, "te")
                done("sqfree%d" % p, "te")
        TE(f_te)
        te_sq = cnt["te"]

        def f_dve2(e):
            w(e, "v", "te", te_sq)
            e.tensor_scalar(out=stat[0:1, 0:TS], in0=ps[7][0:1, 0:TS],
                            scalar1=1.0 / D, scalar2=RMS_EPS,
                            op0=Alu.mult, op1=Alu.add)
            inc(e.reciprocal(out=stat[0:1, 0:TS], in_=stat[0:1, 0:TS]), "v")
            done("ps7", "v")
        V(f_dve2)
        v_r2 = cnt["v"]

        def f_sca2(e):
            w(e, "s", "v", v_r2)
            inc(e.activation(out=stat[0:1, 0:TS], in_=stat[0:1, 0:TS],
                             func=Act.Sqrt), "s")
        S(f_sca2)
        s_r2 = cnt["s"]

        def f_te2(e):
            w(e, "te", "s", s_r2)
            need(e, "te", "ps2")
            inc(e.matmul(ps[2][:, 0:TS], crow[0:1, CO_ONES:CO_ONES + 128],
                         stat[0:1, 0:TS], start=True, stop=True), "te")
        TE(f_te2)
        te_r2 = cnt["te"]

        def f_dve3(e):
            w(e, "v", "te", te_r2)
            for k in range(DT):
                ins = e.scalar_tensor_tensor(
                    out=h2_T[:, k * TS:(k + 1) * TS],
                    in0=ln2_T[:, k * TS:(k + 1) * TS],
                    scalar=nw_s[:, k:k + 1], in1=ps[2][:, 0:TS],
                    op0=Alu.mult, op1=Alu.mult)
            inc(ins, "v")
            done("ps2", "v")
        V(f_dve3)
        new_phase()

        def f_te3(e):
            need(e, "te", "ps6")
            for i in range(TT):
                for k in range(DT):
                    e.matmul(ps[6][:, i * NE:(i + 1) * NE],
                             h2_T[:, k * TS + i * 128:k * TS + (i + 1) * 128],
                             w_gw[:, k * NE:(k + 1) * NE],
                             start=(k == 0), stop=False)
                ins = e.matmul(ps[6][:, i * NE:(i + 1) * NE],
                               crow_bf[0:1, CO_ONES:CO_ONES + 128],
                               crow_bf[0:1, CO_GB:CO_GB + NE],
                               start=False, stop=True)
            inc(ins, "te")
        TE(f_te3)
        te_gl = cnt["te"]

        MO = 3 * TT

        def f_dve4(e):
            w(e, "v", "te", te_gl)
            vsync(e, e.tensor_copy(out=glg[:], in_=ps[6][:, 0:TT * NE]))
            g3 = glg[:].rearrange("p (i e) -> p i e", i=TT)
            m1 = gm[:, 0:TT]
            m2 = gm[:, TT:2 * TT]
            nm1 = gm[:, 2 * TT:3 * TT]
            mskf = gm[:, MO:MO + TT * NE]
            msk2f = gm[:, MO + TT * NE:MO + 2 * TT * NE]
            msk2 = msk2f.rearrange("p (i e) -> p i e", i=TT)
            for i in range(TT):
                ins = e.tensor_reduce(out=m1[:, i:i + 1],
                                      in_=glg[:, i * NE:(i + 1) * NE],
                                      axis=AxX, op=Alu.max)
            vsync(e, ins)
            for i in range(TT):
                ins = e.tensor_tensor(out=mskf[:, i * NE:(i + 1) * NE],
                                      in0=glg[:, i * NE:(i + 1) * NE],
                                      in1=m1[:, i:i + 1].broadcast_to([128, NE]),
                                      op=Alu.is_equal)
            vsync(e, ins)
            vsync(e, e.scalar_tensor_tensor(out=msk2f, in0=mskf, scalar=NEG_BIG,
                                            in1=glg[:], op0=Alu.mult,
                                            op1=Alu.add))
            for i in range(TT):
                ins = e.tensor_reduce(out=m2[:, i:i + 1],
                                      in_=msk2f[:, i * NE:(i + 1) * NE],
                                      axis=AxX, op=Alu.max)
            vsync(e, ins)
            for i in range(TT):
                ins = e.tensor_tensor(out=mskf[:, i * NE:(i + 1) * NE],
                                      in0=glg[:, i * NE:(i + 1) * NE],
                                      in1=m2[:, i:i + 1].broadcast_to([128, NE]),
                                      op=Alu.is_ge)
            vsync(e, ins)
            ins = e.tensor_scalar_mul(out=nm1, in0=m1, scalar1=-1.0)
            inc(ins, "v")
            done("ps6", "v")
        V(f_dve4)
        v_g1 = cnt["v"]

        def f_sca3(e):
            w(e, "s", "v", v_g1)
            nm1 = gm[:, 2 * TT:3 * TT]
            for i in range(TT):
                ins = e.activation(out=glg[:, i * NE:(i + 1) * NE],
                                   in_=glg[:, i * NE:(i + 1) * NE],
                                   func=Act.Exp, bias=nm1[:, i:i + 1])
            inc(ins, "s")
        S(f_sca3)
        s_g = cnt["s"]

        def f_dve5(e):
            w(e, "v", "s", s_g)
            g3 = glg[:].rearrange("p (i e) -> p i e", i=TT)
            mskf = gm[:, MO:MO + TT * NE]
            ssum = gm[:, 0:TT]
            vsync(e, e.tensor_tensor(out=glg[:], in0=glg[:], in1=mskf,
                                     op=Alu.mult))
            for i in range(TT):
                ins = e.tensor_reduce(out=ssum[:, i:i + 1],
                                      in_=glg[:, i * NE:(i + 1) * NE],
                                      axis=AxX, op=Alu.add)
            vsync(e, ins)
            vsync(e, e.reciprocal(out=ssum, in_=ssum))
            for i in range(TT):
                ins = e.tensor_tensor(
                    out=glg[:, i * NE:(i + 1) * NE],
                    in0=glg[:, i * NE:(i + 1) * NE],
                    in1=ssum[:, i:i + 1].broadcast_to([128, NE]), op=Alu.mult)
            inc(ins, "v")
        V(f_dve5)
        v_g2 = cnt["v"]

        def f_te4(e):
            w(e, "te", "v", v_g2)
            need(e, "te", "ps6")
            for i in range(TT):
                ins = e.transpose(ps[6][0:NE, i * 128:(i + 1) * 128],
                                  glg[:, i * NE:(i + 1) * NE], ident[:])
            inc(ins, "te")
        TE(f_te4)
        te_gt = cnt["te"]

        def f_dve6(e):
            w(e, "v", "te", te_gt)
            e.tensor_copy(out=gT[:], in_=ps[6][0:NE, 0:TS])
            inc(e.tensor_scalar_mul(out=gTd[:], in0=gT[:],
                                    scalar1=1.0 / ALPHA), "v")
            done("ps6", "v")
        V(f_dve6)
        new_phase()

        def f_dve7(e):
            for k in range(DT):
                ins = e.tensor_tensor(out=y2[:, k * TS:(k + 1) * TS],
                                      in0=y_T[:, k * TS:(k + 1) * TS],
                                      in1=ln2_T[:, k * TS:(k + 1) * TS],
                                      op=Alu.add)
            inc(ins, "v")
        V(f_dve7)

        b2_prod = []
        v_b20 = cnt["v"]

        def f_te5(e):
            for md in range(DT):
                bank = md % 2
                if md < 2:
                    need(e, "te", "ps%d" % bank)
                else:
                    w(e, "te", "v", v_b20 + (md - 2) + 1)
                ins = e.matmul(ps[bank][:, 0:TS],
                               b2_s[:, md * 128:(md + 1) * 128], gT[:],
                               start=True, stop=True)
                b2_prod.append(inc(ins, "te"))
        TE(f_te5)

        def f_dve8(e):
            for md in range(DT):
                bank = md % 2
                w(e, "v", "te", b2_prod[md])
                ins = e.scalar_tensor_tensor(
                    out=y2[:, md * TS:(md + 1) * TS], in0=ps[bank][:, 0:TS],
                    scalar=1.0, in1=y2[:, md * TS:(md + 1) * TS],
                    op0=Alu.mult, op1=Alu.add)
                inc(ins, "v")
                done("ps%d" % bank, "v")
        V(f_dve8)
        new_phase()

        for e_i in range(NE):
            if sup == 0 and e_i == 0:
                tap("t_glg", glg[:])
                tap("t_h2", h2_T[:])
                tap("t_y2i", y2[:])
            emit_expert(e_i, y2)
            if sup == 0 and e_i == 0:
                tap("t_act", actp[:])
                tap("t_y2e0", y2[:])
        if sup == 0:
            tap("t_y2f", y2[:])
        new_phase()

        ot_prod = []
        v_ot0 = cnt["v"]

        def f_te6(e):
            for k in range(DT):
                bank = 2 + (k % 2)
                if k < 2:
                    need(e, "te", "ps%d" % bank)
                else:
                    w(e, "te", "v", v_ot0 + (k - 2) + 1)
                for i in range(TT):
                    ins = e.transpose(
                        ps[bank][:, i * 128:(i + 1) * 128],
                        y2[:, k * TS + i * 128:k * TS + (i + 1) * 128],
                        ident[:])
                ot_prod.append(inc(ins, "te"))
        TE(f_te6)

        def f_dve9(e):
            for k in range(DT):
                bank = 2 + (k % 2)
                w(e, "v", "te", ot_prod[k])
                for i in range(TT):
                    ins = e.tensor_copy(
                        out=x_tp[:, i * D + k * 128:i * D + (k + 1) * 128],
                        in_=ps[bank][:, i * 128:(i + 1) * 128])
                inc(ins, "v")
                done("ps%d" % bank, "v")
        V(f_dve9)
        v_out = cnt["v"]

        def f_sy(e):
            w(e, "sy", "v", v_out)
            ins = e.dma_start(
                out=dy[t0:t0 + TS, :].rearrange("(i p) d -> p i d", p=128),
                in_=x_tp[:].rearrange("p (i d) -> p i d", i=TT))
            inc(ins, "do", 16)
        SY(f_sy)
        new_phase()

    def emit_expert(e_i, y2):
        w1_prod = []
        w2_prod = [0]
        b1_prod = [0]

        def f_gp(e, e_i=e_i):
            need(e, "g", "ring1")
            for h in range(4):
                ins = e.dma_start(
                    out=ring1[:, h * 2 * I2:(h + 1) * 2 * I2]
                    .rearrange("p (q c) -> p q c", q=2),
                    in_=d_w1[e_i][h * 256:(h + 1) * 256, :]
                    .rearrange("(q p) c -> p q c", p=128))
                w1_prod.append(inc(ins, "dg", 16))
            need(e, "g", "ring2")
            for h in range(4):
                ins = e.dma_start(
                    out=ring2[:, h * 2 * D:(h + 1) * 2 * D]
                    .rearrange("p (q c) -> p q c", q=2),
                    in_=d_w2[e_i][h * 256:(h + 1) * 256, :]
                    .rearrange("(q p) c -> p q c", p=128))
                w2_prod[0] = inc(ins, "dg", 16)
            need(e, "g", "b1row")
            b1_prod[0] = inc(
                e.dma_start(out=crow_bf[0:1, CO_B1:CO_B1 + I2],
                            in_=d_b1[e_i:e_i + 1, :]), "dg", 16)
        G(f_gp)

        gr_prod = [0]

        def f_te0(e, e_i=e_i):
            need(e, "te", "ps4")
            gr_prod[0] = inc(
                e.matmul(ps[4][:, 0:TS], sel[0:NE, e_i * 128:(e_i + 1) * 128],
                         gTd[0:NE, :], start=True, stop=True), "te")
        TE(f_te0)

        m1_prod = {}
        for pr in range(PAIR):
            def f_te(e, pr=pr):
                need(e, "te", "ps0")
                need(e, "te", "ps1")
                w(e, "te", "dg", b1_prod[0])
                for m2i, bank in ((pr, 0), (PAIR + pr, 1)):
                    for k in range(DT):
                        e.matmul(ps[bank][:, 0:TS],
                                 ring1[:, k * I2 + m2i * 128:
                                       k * I2 + (m2i + 1) * 128],
                                 h2_T[:, k * TS:(k + 1) * TS],
                                 start=(k == 0), stop=False)
                    ins = e.matmul(
                        ps[bank][:, 0:TS],
                        crow_bf[0:1, CO_B1 + m2i * 128:CO_B1 + (m2i + 1) * 128],
                        crow_bf[0:1, CO_ONES:CO_ONES + TS],
                        start=False, stop=True)
                    m1_prod[(pr, bank)] = inc(ins, "te")
            TE(f_te)

            vglu = [0]

            def f_dve(e, pr=pr):
                w(e, "v", "te", m1_prod[(pr, 0)])
                need(e, "v", "sg0w")
                vglu[0] = inc(e.tensor_scalar_min(out=sg0[:],
                                                  in0=ps[0][:, 0:TS],
                                                  scalar1=LIMIT), "v")
            V(f_dve)

            ssil = [0]

            def f_sca(e, vglu=vglu):
                w(e, "s", "v", vglu[0])
                need(e, "s", "sg1w")
                ssil[0] = inc(e.activation(out=sg1[:], in_=sg0[:],
                                           func=Act.Silu, scale=ALPHA), "s")
            S(f_sca)

            def f_dve2(e, pr=pr, ssil=ssil):
                w(e, "v", "te", m1_prod[(pr, 1)])
                e.tensor_scalar(out=ab1[:], in0=ps[1][:, 0:TS], scalar1=LIMIT,
                                scalar2=-LIMIT, op0=Alu.min, op1=Alu.max)
                done("ps0", "v")
                done("ps1", "v")
                w(e, "v", "s", ssil[0])
                e.scalar_tensor_tensor(out=sg0[:], in0=ab1[:], scalar=1.0,
                                       in1=sg1[:], op0=Alu.add, op1=Alu.mult)
                done("sg1w", "v")
                w(e, "v", "te", gr_prod[0])
                ins = e.tensor_tensor(out=actp[:, pr * TS:(pr + 1) * TS],
                                      in0=sg0[:], in1=ps[4][:, 0:TS],
                                      op=Alu.mult)
                inc(ins, "v")
                done("sg0w", "v")
            V(f_dve2)
        done("ps4", "v")
        v_act = cnt["v"]

        m2_prod = []
        v_m20 = cnt["v"]

        def f_te2(e, e_i=e_i):
            w(e, "te", "v", v_act)
            w(e, "te", "dg", b1_prod[0])
            for md in range(DT):
                bank = md % 2
                if md < 2:
                    need(e, "te", "ps%d" % bank)
                else:
                    w(e, "te", "v", v_m20 + (md - 2) + 1)
                for k in range(IIT):
                    ins = e.matmul(ps[bank][:, 0:TS],
                                   ring2[:, k * D + md * 128:
                                         k * D + (md + 1) * 128],
                                   actp[:, k * TS:(k + 1) * TS],
                                   start=(k == 0), stop=(k == IIT - 1))
                m2_prod.append(inc(ins, "te"))
        TE(f_te2)

        def f_dve3(e):
            for md in range(DT):
                bank = md % 2
                w(e, "v", "te", m2_prod[md])
                ins = e.scalar_tensor_tensor(
                    out=y2[:, md * TS:(md + 1) * TS], in0=ps[bank][:, 0:TS],
                    scalar=1.0, in1=y2[:, md * TS:(md + 1) * TS],
                    op0=Alu.mult, op1=Alu.add)
                inc(ins, "v")
                done("ps%d" % bank, "v")
        V(f_dve3)
        done("ring1", "v")
        done("ring2", "v")
        done("b1row", "te")

    for sup in range(NSUP):
        emit_load_transpose(sup, prefix=True)
        emit_ln(x_T, ln_T)
        emit_dt_u(sup, prefix=True)
        emit_bc_proj(sup, prefix=True)
        emit_uy(sup, prefix=True)
        emit_scan_prefix(sup)
    for sup in range(NSUP):
        emit_load_transpose(sup, prefix=False)
        if sup == 0:
            tap("t_xT", x_T[:])
        emit_ln(x_T, ln_T)
        if sup == 0:
            tap("t_lnT", ln_T[:])
            tap("t_stat", stat[:])
        emit_dt_u(sup, prefix=False)
        if sup == 0:
            tap("t_dtT", dt_T[:])
            tap("t_uT", u_T[:])
        emit_bc_proj(sup, prefix=False)
        emit_uy(sup, prefix=False)
        emit_scan_main(sup)
        if sup == 0:
            tap("t_yscan", y_T[:])
            tap("t_hst", hst[:])
        emit_post_ssm(sup)

    def f_fin(e):
        w(e, "sy", "do", cnt["do"])
    SY(f_fin)

    es.close()
    return nc


def _prep(inputs):
    perm = np.concatenate([np.arange(0, I2, 2), np.arange(1, I2, 2)])
    w = {}
    w["dtwT"] = np.ascontiguousarray(inputs["L0_attn_dt_w"].T)
    w["bwT"] = np.ascontiguousarray(inputs["L0_attn_B_w"].T)
    w["cwT"] = np.ascontiguousarray(inputs["L0_attn_C_w"].T)
    w["gwT"] = np.ascontiguousarray(inputs["L0_ffn_gate_w"].T)
    w["logA"] = np.ascontiguousarray(inputs["L0_attn_log_A"])
    w["dtb"] = np.ascontiguousarray(inputs["L0_attn_dt_b"].reshape(1, D))
    w["bb"] = np.ascontiguousarray(inputs["L0_attn_B_b"].reshape(1, NST))
    w["cb"] = np.ascontiguousarray(inputs["L0_attn_C_b"].reshape(1, NST))
    w["gb"] = np.ascontiguousarray(inputs["L0_ffn_gate_b"].reshape(1, NE))
    w["dsk"] = np.ascontiguousarray(inputs["L0_attn_D_skip"].reshape(D, 1))
    w["nw"] = np.ascontiguousarray(inputs["L0_ffn_norm_w"].reshape(D, 1))
    w["w1T"] = np.ascontiguousarray(
        inputs["L0_ffn_mlp1_w"].transpose(0, 2, 1)[:, :, perm])
    w["b1g"] = np.ascontiguousarray(inputs["L0_ffn_mlp1_b"][:, perm])
    w["w2T"] = np.ascontiguousarray(inputs["L0_ffn_mlp2_w"].transpose(0, 2, 1))
    w["b2"] = np.ascontiguousarray(inputs["L0_ffn_mlp2_b"])
    for k in w:
        w[k] = np.asarray(w[k], dtype=np.float32)
    return w


def kernel(**inputs):
    x = np.ascontiguousarray(np.asarray(inputs["L0_attn_x"], dtype=np.float32))
    w = _prep(inputs)
    zeros = np.zeros((TOK, D), dtype=np.float32)
    in_maps = []
    for c in range(8):
        b, half = c // 2, c % 2
        m = dict(w)
        m["x_own"] = np.ascontiguousarray(x[b, half * TOK:(half + 1) * TOK, :])
        m["x_pre"] = np.ascontiguousarray(x[b, 0:TOK, :]) if half == 1 else zeros
        in_maps.append(m)
    nc = build_nc()
    res = run_bass_kernel_spmd(nc, in_maps, core_ids=list(range(8)))
    out = np.empty((B, T, D), dtype=np.float32)
    for c in range(8):
        b, half = c // 2, c % 2
        out[b, half * TOK:(half + 1) * TOK, :] = np.asarray(res.results[c]["y"])
    return out


# version 12
# speedup vs baseline: 1.0985x; 1.0985x over previous
import contextlib

import numpy as np

import concourse.bass as bass
import concourse.mybir as mybir
from concourse.bass_utils import run_bass_kernel_spmd

F32 = mybir.dt.float32
BF16 = mybir.dt.bfloat16
Act = mybir.ActivationFunctionType
Alu = mybir.AluOpType
AxX = mybir.AxisListType.X

D = 1024
NST = 32
NE = 8
I2 = 2048
II = 1024
B = 4
T = 8192
TOK = 4096
TS = 512
NSUP = TOK // TS
DT = D // 128
TT = TS // 128
IIT = II // 128
PAIR = 8
LN_EPS = 1e-5
RMS_EPS = 2.220446049250313e-16
ALPHA = 1.702
LIMIT = 7.0
NEG_BIG = -3.0e38

DEBUG_TAPS = False

CO_ONES = 0
CO_DTB = 512
CO_BB = 1536
CO_CB = 1568
CO_GB = 1600
CO_B1 = 2048


def build_nc():
    nc = bass.Bass("TRN2", target_bir_lowering=False, debug=False,
                   detect_race_conditions=False)
    es = contextlib.ExitStack()

    def dr(name, shape, kind="ExternalInput"):
        return nc.dram_tensor(name, shape, F32, kind=kind).ap()

    dx = dr("x_own", [TOK, D])
    dxp = dr("x_pre", [TOK, D])
    d_dtw = dr("dtwT", [D, D])
    d_bw = dr("bwT", [D, NST])
    d_cw = dr("cwT", [D, NST])
    d_gw = dr("gwT", [D, NE])
    d_la = dr("logA", [D, NST])
    d_dtb = dr("dtb", [1, D])
    d_bb = dr("bb", [1, NST])
    d_cb = dr("cb", [1, NST])
    d_gb = dr("gb", [1, NE])
    d_dsk = dr("dsk", [D, 1])
    d_nw = dr("nw", [D, 1])
    d_w1 = dr("w1T", [NE, D, I2])
    d_b1 = dr("b1g", [NE, I2])
    d_w2 = dr("w2T", [NE, II, D])
    d_b2 = dr("b2", [NE, D])
    dy = dr("y", [TOK, D], kind="ExternalOutput")
    taps = {}
    if DEBUG_TAPS:
        for tn, shape in (("t_xT", [128, DT * TS]), ("t_lnT", [128, DT * TS]),
                          ("t_dtT", [128, DT * TS]), ("t_uT", [128, DT * TS]),
                          ("t_yscan", [128, DT * TS]), ("t_stat", [1, 2 * TS]),
                          ("t_glg", [128, TT * NE]), ("t_hst", [128, DT * NST]),
                          ("t_y2i", [128, DT * TS]), ("t_y2e0", [128, DT * TS]),
                          ("t_act", [128, IIT * TS]), ("t_y2f", [128, DT * TS]),
                          ("t_h2", [128, DT * TS])):
            taps[tn] = dr(tn, shape, kind="ExternalOutput")

    def sbt(name, shape, dt):
        return es.enter_context(nc.sbuf_tensor(name, shape, dt))

    x_tp = sbt("x_tp", [128, TT * D], F32)
    x_T = sbt("x_T", [128, DT * TS], F32)
    ln_T = sbt("ln_T", [128, DT * TS], BF16)
    ln2t = sbt("ln2t", [128, DT * TS], BF16)
    actt = sbt("actt", [128, IIT * TS], BF16)
    dt_T = sbt("dt_T", [128, DT * TS], F32)
    y_T = sbt("y_T", [128, DT * TS], F32)
    ub_s = sbt("ub_s", [128, DT * TS], BF16)
    brep_b = sbt("brep_b", [128, TS], BF16)
    ab0 = sbt("ab0", [128, TS], F32)
    ab1 = sbt("ab1", [128, TS], F32)
    sg0 = sbt("sg0", [128, TS], F32)
    sg1 = sbt("sg1", [128, TS], F32)
    ring1 = sbt("ring1", [128, DT * I2], BF16)
    ring2 = sbt("ring2", [128, IIT * D], BF16)
    w_bw = sbt("w_bw", [128, DT * NST], BF16)
    w_cw = sbt("w_cw", [128, DT * NST], BF16)
    w_gw = sbt("w_gw", [128, DT * NE], BF16)
    a_s = sbt("a_s", [128, DT * NST], F32)
    na_s = sbt("na_s", [128, DT * NST], F32)
    hst = sbt("hst", [128, DT * NST], F32)
    hloc = sbt("hloc", [128, DT * NST], F32)
    dsk_s = sbt("dsk_s", [128, DT], F32)
    dtb_s = sbt("dtb_s", [128, DT], F32)
    nw_s = sbt("nw_s", [128, DT], F32)
    bpp = sbt("bpp", [32, TS], BF16)
    cpp = sbt("cpp", [32, TS], BF16)
    sel = sbt("sel", [32, NST * 128], BF16)
    glg = sbt("glg", [128, TT * NE], F32)
    gm = sbt("gm", [128, 3 * TT + 2 * TT * NE], F32)
    gT = sbt("gT", [8, TS], BF16)
    gTd = sbt("gTd", [8, TS], BF16)
    cneg = sbt("cneg", [128, TT * NE], F32)
    stat = sbt("stat", [1, 3 * TS], F32)
    crow = sbt("crow", [1, 2048], F32)
    crow_bf = sbt("crow_bf", [1, 4096], BF16)
    b2_s = sbt("b2_s", [8, D], BF16)
    onec = sbt("onec", [128, 1], F32)
    ident = sbt("ident", [128, 128], F32)
    ds_col = sbt("ds_col", [128, DT], F32)
    dec_s = sbt("dec_s", [128, NST], F32)
    bcol = sbt("bcol", [128, DT], F32)

    ps = [es.enter_context(nc.psum_tensor("psb%d" % i, [128, 512], F32))
          for i in range(8)]

    ln2_T = ln2t
    h2_T = ln_T
    actp = actt
    dtw_f = ring1[:, 0:DT * D]
    u_T = ln_T

    sems = {}
    for name in ("dx", "dw", "dg", "te", "v", "s", "g", "do"):
        sems[name] = es.enter_context(nc.semaphore(name))
    cnt = {k: 0 for k in sems}
    waited = {}

    def w(eng, ename, sem, val=None):
        v = cnt[sem] if val is None else val
        if v <= 0:
            return
        if waited.get((ename, sem), -1) >= v:
            return
        waited[(ename, sem)] = v
        eng.wait_ge(sems[sem], v)

    def inc(ins, sem, n=1):
        cnt[sem] += n
        ins.then_inc(sems[sem], n)
        return cnt[sem]

    dep = {}

    def vsync(e, ins):
        v_ = inc(ins, "v")
        w(e, "v", "v", v_)

    def need(eng, ename, res):
        if res in dep:
            s_, v_ = dep[res]
            w(eng, ename, s_, v_)

    def done(res, sem):
        dep[res] = (sem, cnt[sem])

    block = es.enter_context(nc.Block())

    bar = {k: 0 for k in cnt}

    def new_phase():
        for k in cnt:
            bar[k] = cnt[k]

    def _wrap(ename, f):
        def g(e):
            for k, v in bar.items():
                w(e, ename, k, v)
            f(e)
        return g

    def TE(f):
        block.tensor(_wrap("te", f))

    def V(f):
        block.vector(_wrap("v", f))

    def S(f):
        block.scalar(_wrap("s", f))

    def G(f):
        block.gpsimd(_wrap("g", f))

    def SY(f):
        block.sync(_wrap("sy", f))

    def su_sync(e):
        inc(e.dma_start(out=a_s[:].rearrange("p (k n) -> p k n", k=DT),
                        in_=d_la.rearrange("(k p) n -> p k n", p=128)), "dw", 16)
        inc(e.dma_start(out=dsk_s[:, :, None],
                        in_=d_dsk.rearrange("(k p) o -> p k o", p=128),
                        allow_slow_non_contiguous=True), "dw", 16)
        inc(e.dma_start(out=dtb_s[:, :, None],
                        in_=d_dtb.rearrange("o (k p) -> p k o", p=128),
                        allow_slow_non_contiguous=True), "dw", 16)
        inc(e.dma_start(out=nw_s[:, :, None],
                        in_=d_nw.rearrange("(k p) o -> p k o", p=128),
                        allow_slow_non_contiguous=True), "dw", 16)
    SY(su_sync)

    def su_sync1b(e):
        inc(e.dma_start(out=crow[0:1, CO_DTB:CO_DTB + D], in_=d_dtb), "dw", 16)
        inc(e.dma_start(out=crow[0:1, CO_BB:CO_BB + NST], in_=d_bb), "dw", 16)
        inc(e.dma_start(out=crow[0:1, CO_CB:CO_CB + NST], in_=d_cb), "dw", 16)
    SY(su_sync1b)


    def su_gp(e):
        inc(e.dma_start(out=w_bw[:].rearrange("p (k n) -> p k n", k=DT),
                        in_=d_bw.rearrange("(k p) n -> p k n", p=128)), "dg", 16)
        inc(e.dma_start(out=w_cw[:].rearrange("p (k n) -> p k n", k=DT),
                        in_=d_cw.rearrange("(k p) n -> p k n", p=128)), "dg", 16)
        inc(e.dma_start(out=w_gw[:].rearrange("p (k n) -> p k n", k=DT),
                        in_=d_gw.rearrange("(k p) n -> p k n", p=128)), "dg", 16)
        inc(e.dma_start(out=crow_bf[0:1, CO_DTB:CO_DTB + D], in_=d_dtb), "dg", 16)
        inc(e.dma_start(out=crow_bf[0:1, CO_BB:CO_BB + NST], in_=d_bb), "dg", 16)
        inc(e.dma_start(out=crow_bf[0:1, CO_CB:CO_CB + NST], in_=d_cb), "dg", 16)
        inc(e.dma_start(out=crow_bf[0:1, CO_GB:CO_GB + NE], in_=d_gb), "dg", 16)
        inc(e.dma_start(out=b2_s[:], in_=d_b2), "dg", 16)
        inc(e.memset(onec[:], 1.0), "g")
        inc(e.memset(crow[0:1, 0:512], 1.0), "g")
        inc(e.memset(crow_bf[0:1, CO_ONES:CO_ONES + 512], 1.0), "g")
        inc(e.memset(hst[:], 0.0), "g")
        inc(e.memset(cneg[:], NEG_BIG), "g")
    G(su_gp)
    g_ms = cnt["g"]

    def su_iota(e):
        w(e, "v", "g", g_ms)
        ins = e.tensor_tensor_scan(
            out=stat[0:1, 0:128],
            data0=onec[0:1, 0:1].broadcast_to([1, 128]),
            data1=onec[0:1, 0:1].broadcast_to([1, 128]),
            initial=-1.0, op0=Alu.mult, op1=Alu.add)
        inc(ins, "v")
    V(su_iota)
    v_iota = cnt["v"]

    def su_te(e):
        w(e, "te", "v", v_iota)
        e.matmul(ps[6][:, 0:1], stat[0:1, 0:128], onec[0:1, 0:1],
                 start=True, stop=True)
        inc(e.matmul(ps[7][:, 0:128], crow[0:1, 0:128], stat[0:1, 0:128],
                     start=True, stop=True), "te")
    TE(su_te)
    te_iota = cnt["te"]

    def su_dve0(e):
        w(e, "v", "te", te_iota)
        vsync(e, e.tensor_copy(out=bcol[:, 0:1], in_=ps[6][:, 0:1]))
        vsync(e, e.tensor_copy(out=x_tp[:, 0:128], in_=ps[7][:, 0:128]))
        vsync(e, e.tensor_tensor(out=ident[:],
                                 in0=bcol[:, 0:1].broadcast_to([128, 128]),
                                 in1=x_tp[:, 0:128], op=Alu.is_equal))
        for n in range(NST):
            ins = e.tensor_scalar(
                out=sel[0:32, n * 128:(n + 1) * 128],
                in0=bcol[0:32, 0:1].broadcast_to([32, 128]),
                scalar1=float(n), scalar2=None, op0=Alu.is_equal)
        inc(ins, "v")
        done("ps6", "v")
        done("ps7", "v")
    V(su_dve0)

    def su_sca(e):
        w(e, "s", "dw", cnt["dw"])
        inc(e.activation(out=na_s[:], in_=a_s[:], func=Act.Exp), "s")
    S(su_sca)

    def su_dve(e):
        w(e, "v", "s", cnt["s"])
        inc(e.tensor_scalar_mul(out=a_s[:], in0=na_s[:], scalar1=-1.0), "v")
    V(su_dve)
    new_phase()

    def emit_ln(src_T, dst_T):
        def f_te(e):
            need(e, "te", "ps6")
            for k in range(DT):
                ins = e.matmul(ps[6][0:1, 0:TS], onec[:],
                               src_T[:, k * TS:(k + 1) * TS],
                               start=(k == 0), stop=(k == DT - 1))
            inc(ins, "te")
        TE(f_te)

        te_sq0 = cnt["te"] + 1

        def f_sca(e):
            for k in range(DT):
                p = k % 2
                if k < 2:
                    need(e, "s", "sqfree%d" % p)
                else:
                    w(e, "s", "te", te_sq0 + (k - 2) + 1)
                ins = e.activation(out=(sg0 if p == 0 else sg1)[:],
                                   in_=src_T[:, k * TS:(k + 1) * TS],
                                   func=Act.Square)
                inc(ins, "s")
        S(f_sca)

        sq_first = cnt["s"] - DT

        def f_te2b(e):
            need(e, "te", "ps7")
            for k in range(DT):
                p = k % 2
                w(e, "te", "s", sq_first + k + 1)
                ins = e.matmul(ps[7][0:1, 0:TS], onec[:],
                               (sg0 if p == 0 else sg1)[:],
                               start=(k == 0), stop=(k == DT - 1))
                inc(ins, "te")
                done("sqfree%d" % p, "te")
        TE(f_te2b)
        te_stats = cnt["te"]

        def f_dve(e):
            w(e, "v", "te", te_stats)
            vsync(e, e.tensor_scalar_mul(out=stat[0:1, 0:TS],
                                         in0=ps[6][0:1, 0:TS], scalar1=1.0 / D))
            vsync(e, e.tensor_scalar_mul(out=stat[0:1, TS:2 * TS],
                                         in0=ps[7][0:1, 0:TS], scalar1=1.0 / D))
            vsync(e, e.tensor_tensor(out=stat[0:1, 2 * TS:3 * TS],
                                     in0=stat[0:1, 0:TS],
                                     in1=stat[0:1, 0:TS], op=Alu.mult))
            vsync(e, e.tensor_tensor(out=stat[0:1, TS:2 * TS],
                                     in0=stat[0:1, TS:2 * TS],
                                     in1=stat[0:1, 2 * TS:3 * TS],
                                     op=Alu.subtract))
            vsync(e, e.tensor_scalar_add(out=stat[0:1, TS:2 * TS],
                                         in0=stat[0:1, TS:2 * TS],
                                         scalar1=LN_EPS))
            ins = e.reciprocal(out=stat[0:1, TS:2 * TS],
                               in_=stat[0:1, TS:2 * TS])
            inc(ins, "v")
            done("ps6", "v")
            done("ps7", "v")
        V(f_dve)
        v_st = cnt["v"]

        def f_sca2(e):
            w(e, "s", "v", v_st)
            inc(e.activation(out=stat[0:1, TS:2 * TS],
                             in_=stat[0:1, TS:2 * TS], func=Act.Sqrt), "s")
        S(f_sca2)
        s_rstd = cnt["s"]

        def f_dve2(e):
            w(e, "v", "s", s_rstd)
            inc(e.tensor_tensor(out=stat[0:1, 0:TS], in0=stat[0:1, 0:TS],
                                in1=stat[0:1, TS:2 * TS], op=Alu.mult), "v")
        V(f_dve2)
        v_nu = cnt["v"]

        def f_te3(e):
            w(e, "te", "v", v_nu)
            need(e, "te", "ps2")
            need(e, "te", "ps3")
            e.matmul(ps[2][:, 0:TS], crow[0:1, CO_ONES:CO_ONES + 128],
                     stat[0:1, TS:2 * TS], start=True, stop=True)
            inc(e.matmul(ps[3][:, 0:TS], crow[0:1, CO_ONES:CO_ONES + 128],
                         stat[0:1, 0:TS], start=True, stop=True), "te")
        TE(f_te3)
        te_rep = cnt["te"]

        def f_dve3(e):
            w(e, "v", "te", te_rep)
            for k in range(DT):
                e.tensor_tensor(out=sg0[:], in0=src_T[:, k * TS:(k + 1) * TS],
                                in1=ps[2][:, 0:TS], op=Alu.mult)
                ins = e.tensor_tensor(out=dst_T[:, k * TS:(k + 1) * TS],
                                      in0=sg0[:], in1=ps[3][:, 0:TS],
                                      op=Alu.subtract)
            inc(ins, "v")
            done("ps2", "v")
            done("ps3", "v")
            done("sqfree0", "v")
            done("sqfree1", "v")
        V(f_dve3)
        new_phase()

    dtw_dg = [0]
    def tap(name, src_ap):
        if not DEBUG_TAPS or name not in taps:
            return
        new_phase()
        def f_g(e):
            ins = e.dma_start(out=taps[name], in_=src_ap)
            inc(ins, "dg", 16)
        G(f_g)
        new_phase()

    def emit_load_transpose(sup, prefix):
        xsrc = dxp if prefix else dx
        t0 = sup * TS

        def f_gdtw(e):
            need(e, "g", "ring1")
            inc(e.dma_start(out=dtw_f.rearrange("p (k c) -> p k c", k=DT),
                            in_=d_dtw.rearrange("(k p) c -> p k c", p=128)),
                "dg", 16)
        G(f_gdtw)
        dtw_dg[0] = cnt["dg"]

        def f_sy(e):
            ins = e.dma_start(
                out=x_tp[:].rearrange("p (i d) -> p i d", i=TT),
                in_=xsrc[t0:t0 + TS, :].rearrange("(i p) d -> p i d", p=128))
            inc(ins, "dx", 16)
        SY(f_sy)
        dxv = cnt["dx"]

        tx_prod = []
        v_evac0 = cnt["v"]

        def f_te(e):
            w(e, "te", "dx", dxv)
            for k in range(DT):
                bank = 2 + (k % 2)
                if k < 2:
                    need(e, "te", "ps%d" % bank)
                else:
                    w(e, "te", "v", v_evac0 + (k - 2) + 1)
                for i in range(TT):
                    ins = e.transpose(
                        ps[bank][:, i * 128:(i + 1) * 128],
                        x_tp[:, i * D + k * 128:i * D + (k + 1) * 128],
                        ident[:])
                tx_prod.append(inc(ins, "te"))
        TE(f_te)

        def f_dve(e):
            for k in range(DT):
                bank = 2 + (k % 2)
                w(e, "v", "te", tx_prod[k])
                ins = e.tensor_copy(out=x_T[:, k * TS:(k + 1) * TS],
                                    in_=ps[bank][:, 0:TS])
                inc(ins, "v")
                done("ps%d" % bank, "v")
        V(f_dve)
        new_phase()

    def emit_dt_u(sup, prefix):
        dtp_prod = []
        s_dt0 = cnt["s"]

        def f_te(e):
            w(e, "te", "dg", dtw_dg[0])
            for m in range(DT):
                bank = m % 2
                if m < 2:
                    need(e, "te", "ps%d" % bank)
                else:
                    w(e, "te", "s", s_dt0 + (m - 2) + 1)
                for k in range(DT):
                    ins = e.matmul(ps[bank][:, 0:TS],
                                   dtw_f[:, k * D + m * 128:k * D + (m + 1) * 128],
                                   ln_T[:, k * TS:(k + 1) * TS],
                                   start=(k == 0), stop=(k == DT - 1))
                dtp_prod.append(inc(ins, "te"))
            done("ring1", "te")
        TE(f_te)

        sp_prod = []

        def f_sca(e):
            for m in range(DT):
                bank = m % 2
                w(e, "s", "te", dtp_prod[m])
                e.activation(out=sg1[:], in_=ps[bank][:, 0:TS], func=Act.Exp,
                             bias=dtb_s[:, m:m + 1])
                ins = e.activation(out=dt_T[:, m * TS:(m + 1) * TS], in_=sg1[:],
                                   func=Act.Ln, bias=1.0)
                sp_prod.append(inc(ins, "s"))
                done("ps%d" % bank, "s")
        S(f_sca)

        new_phase()

    def emit_bc_proj(sup, prefix):
        def f_te(e):
            need(e, "te", "ps0")
            need(e, "te", "ps1")
            for k in range(DT):
                e.matmul(ps[0][0:NST, 0:TS],
                         w_bw[:, k * NST:(k + 1) * NST],
                         ln_T[:, k * TS:(k + 1) * TS],
                         start=(k == 0), stop=False)
            ins = e.matmul(ps[0][0:NST, 0:TS], crow_bf[0:1, CO_BB:CO_BB + NST],
                           crow_bf[0:1, CO_ONES:CO_ONES + TS],
                           start=False, stop=True)
            if not prefix:
                for k in range(DT):
                    e.matmul(ps[1][0:NST, 0:TS],
                             w_cw[:, k * NST:(k + 1) * NST],
                             ln_T[:, k * TS:(k + 1) * TS],
                             start=(k == 0), stop=False)
                ins = e.matmul(ps[1][0:NST, 0:TS], crow_bf[0:1, CO_CB:CO_CB + NST],
                               crow_bf[0:1, CO_ONES:CO_ONES + TS],
                               start=False, stop=True)
            inc(ins, "te")
        TE(f_te)
        te_bc = cnt["te"]

        def f_dve(e):
            w(e, "v", "te", te_bc)
            ins = e.tensor_copy(out=bpp[:], in_=ps[0][0:NST, 0:TS])
            if not prefix:
                ins = e.tensor_copy(out=cpp[:], in_=ps[1][0:NST, 0:TS])
            inc(ins, "v")
            done("ps0", "v")
            done("ps1", "v")
        V(f_dve)
        new_phase()

    def emit_uy(sup, prefix):
        def f_dve(e):
            for m in range(DT):
                if not prefix:
                    e.tensor_scalar(out=y_T[:, m * TS:(m + 1) * TS],
                                    in0=ln_T[:, m * TS:(m + 1) * TS],
                                    scalar1=dsk_s[:, m:m + 1],
                                    scalar2=None, op0=Alu.mult)
                ins = e.tensor_tensor(out=ln_T[:, m * TS:(m + 1) * TS],
                                      in0=dt_T[:, m * TS:(m + 1) * TS],
                                      in1=ln_T[:, m * TS:(m + 1) * TS],
                                      op=Alu.mult)
            inc(ins, "v")
        V(f_dve)
        new_phase()

    def emit_scan_main(sup):
        for n in range(NST):
            pb = 4 if n % 2 == 0 else 2
            pc = 5 if n % 2 == 0 else 3
            te_rep = [0]

            def f_te(e, n=n, pb=pb, pc=pc, te_rep=te_rep):
                need(e, "te", "ps%d" % pb)
                need(e, "te", "ps%d" % pc)
                e.matmul(ps[pb][:, 0:TS], sel[:, n * 128:(n + 1) * 128],
                         bpp[:], start=True, stop=True)
                te_rep[0] = inc(
                    e.matmul(ps[pc][:, 0:TS], sel[:, n * 128:(n + 1) * 128],
                             cpp[:], start=True, stop=True), "te")
            TE(f_te)

            v0 = cnt["v"]
            ab_prod = []
            sb_ev = [0]

            def f_sca(e, n=n, v0=v0, ab_prod=ab_prod, sb_ev=sb_ev,
                      te_rep=te_rep):
                w(e, "s", "te", te_rep[0])
                need(e, "s", "brepb")
                sb_ev[0] = inc(e.activation(out=brep_b[:], in_=ps[pb][:, 0:TS],
                                            func=Act.Copy), "s")
                done("ps%d" % pb, "s")
                for dblk in range(DT):
                    p = dblk % 2
                    ab = ab0 if p == 0 else ab1
                    w(e, "s", "v", v0 if dblk < 2 else v0 + dblk - 1)
                    ins = e.activation(
                        out=ab[:], in_=dt_T[:, dblk * TS:(dblk + 1) * TS],
                        func=Act.Exp,
                        scale=a_s[:, dblk * NST + n:dblk * NST + n + 1])
                    ab_prod.append(inc(ins, "s"))
            S(f_sca)

            def f_dve(e, n=n, pb=pb, pc=pc, te_rep=te_rep, ab_prod=ab_prod,
                      sb_ev=sb_ev):
                w(e, "v", "s", sb_ev[0])
                for dblk in range(DT):
                    e.tensor_tensor(
                        out=ub_s[:, dblk * TS:(dblk + 1) * TS],
                        in0=u_T[:, dblk * TS:(dblk + 1) * TS],
                        in1=brep_b[:], op=Alu.mult)
                sv = 0
                for dblk in range(DT):
                    p = dblk % 2
                    ab = ab0 if p == 0 else ab1
                    w(e, "v", "s", ab_prod[dblk])
                    col = dblk * NST + n
                    sv = inc(e.tensor_tensor_scan(
                        out=x_tp[:, dblk * TS:(dblk + 1) * TS],
                        data0=ab[:], data1=ub_s[:, dblk * TS:(dblk + 1) * TS],
                        initial=hst[:, col:col + 1],
                        op0=Alu.mult, op1=Alu.add), "v")
                w(e, "v", "v", sv)
                ins = e.tensor_copy(
                    out=hst[:, n::NST][:, 0:DT],
                    in_=x_tp[:, TS - 1::TS][:, 0:DT])
                inc(ins, "v")
                for dblk in range(DT):
                    e.tensor_tensor(
                        out=ub_s[:, dblk * TS:(dblk + 1) * TS],
                        in0=x_tp[:, dblk * TS:(dblk + 1) * TS],
                        in1=ps[pc][:, 0:TS], op=Alu.mult)
                ins = e.tensor_tensor(out=y_T[:], in0=y_T[:], in1=ub_s[:],
                                      op=Alu.add)
                inc(ins, "v")
                done("brepb", "v")
                done("ps%d" % pc, "v")
            V(f_dve)
        new_phase()

    def emit_scan_prefix(sup):
        def f_dve(e):
            for dblk in range(DT):
                sv = inc(e.tensor_tensor_scan(
                    out=y_T[:, dblk * TS:(dblk + 1) * TS],
                    data0=onec[:, 0:1].broadcast_to([128, TS]),
                    data1=dt_T[:, dblk * TS:(dblk + 1) * TS],
                    initial=0.0, op0=Alu.mult, op1=Alu.add), "v")
                w(e, "v", "v", sv)
                ins = e.tensor_copy(
                    out=ds_col[:, dblk:dblk + 1],
                    in_=y_T[:, (dblk + 1) * TS - 1:(dblk + 1) * TS])
            inc(ins, "v")
        V(f_dve)
        new_phase()

        prev_f_end = [0]
        for n in range(NST):
            pb = 4 if n % 2 == 0 else 2
            te_rep = [0]

            def f_te(e, n=n, pb=pb, te_rep=te_rep):
                need(e, "te", "ps%d" % pb)
                te_rep[0] = inc(
                    e.matmul(ps[pb][:, 0:TS], sel[:, n * 128:(n + 1) * 128],
                             bpp[:], start=True, stop=True), "te")
            TE(f_te)

            bc_prod = []
            sb_ev2 = [0]

            def f_scev2(e, pb=pb, te_rep=te_rep):
                w(e, "s", "te", te_rep[0])
                need(e, "s", "brepb")
                sb_ev2[0] = inc(e.activation(out=brep_b[:], in_=ps[pb][:, 0:TS],
                                             func=Act.Copy), "s")
                done("ps%d" % pb, "s")
            S(f_scev2)

            def f_dve2(e, n=n, te_rep=te_rep, bc_prod=bc_prod, sb_ev2=sb_ev2):
                w(e, "v", "s", sb_ev2[0])
                w(e, "v", "s", prev_f_end[0])
                for dblk in range(DT):
                    ins = e.tensor_tensor(
                        out=ub_s[:, dblk * TS:(dblk + 1) * TS],
                        in0=u_T[:, dblk * TS:(dblk + 1) * TS],
                        in1=brep_b[:], op=Alu.mult)
                inc(ins, "v")
                done("brepb", "v")
                for dblk in range(DT):
                    ins = e.tensor_scalar(
                        out=bcol[:, dblk:dblk + 1],
                        in0=a_s[:, dblk * NST + n:dblk * NST + n + 1],
                        scalar1=ds_col[:, dblk:dblk + 1], scalar2=None,
                        op0=Alu.mult)
                    bc_prod.append(inc(ins, "v"))
            V(f_dve2)

            v1 = cnt["v"]
            f_prod = []

            def f_sca(e, n=n, bc_prod=bc_prod, f_prod=f_prod, v1=v1):
                for dblk in range(DT):
                    p = dblk % 2
                    ab = ab0 if p == 0 else ab1
                    w(e, "s", "v", bc_prod[dblk])
                    if dblk >= 2:
                        w(e, "s", "v", v1 + dblk - 1)
                    ins = e.activation(
                        out=ab[:], in_=y_T[:, dblk * TS:(dblk + 1) * TS],
                        func=Act.Exp,
                        scale=na_s[:, dblk * NST + n:dblk * NST + n + 1],
                        bias=bcol[:, dblk:dblk + 1])
                    f_prod.append(inc(ins, "s"))
            S(f_sca)
            prev_f_end[0] = cnt["s"]

            def f_dve3(e, n=n, f_prod=f_prod):
                for dblk in range(DT):
                    p = dblk % 2
                    ab = ab0 if p == 0 else ab1
                    w(e, "v", "s", f_prod[dblk])
                    col = dblk * NST + n
                    ins = e.scalar_tensor_tensor(
                        out=sg0[:], in0=ab[:], scalar=1.0,
                        in1=ub_s[:, dblk * TS:(dblk + 1) * TS],
                        op0=Alu.mult, op1=Alu.mult,
                        accum_out=hloc[:, col:col + 1])
                    inc(ins, "v")
            V(f_dve3)
        new_phase()

        for dblk in range(DT):
            vd = [0]

            def f_dve4(e, dblk=dblk, vd=vd):
                ins = e.tensor_scalar(
                    out=dec_s[:], in0=a_s[:, dblk * NST:(dblk + 1) * NST],
                    scalar1=ds_col[:, dblk:dblk + 1], scalar2=None,
                    op0=Alu.mult)
                vd[0] = inc(ins, "v")
            V(f_dve4)
            sd = [0]

            def f_sca2(e, vd=vd, sd=sd):
                w(e, "s", "v", vd[0])
                sd[0] = inc(e.activation(out=dec_s[:], in_=dec_s[:],
                                         func=Act.Exp), "s")
            S(f_sca2)

            def f_dve5(e, dblk=dblk, sd=sd):
                w(e, "v", "s", sd[0])
                vsync(e, e.tensor_tensor(
                    out=hst[:, dblk * NST:(dblk + 1) * NST],
                    in0=hst[:, dblk * NST:(dblk + 1) * NST],
                    in1=dec_s[:], op=Alu.mult))
                ins = e.tensor_tensor(out=hst[:, dblk * NST:(dblk + 1) * NST],
                                      in0=hst[:, dblk * NST:(dblk + 1) * NST],
                                      in1=hloc[:, dblk * NST:(dblk + 1) * NST],
                                      op=Alu.add)
                inc(ins, "v")
            V(f_dve5)
        new_phase()

    def emit_post_ssm(sup):
        t0 = sup * TS
        y2 = dt_T

        def f_dve(e):
            for k in range(DT):
                ins = e.tensor_tensor(out=y_T[:, k * TS:(k + 1) * TS],
                                      in0=y_T[:, k * TS:(k + 1) * TS],
                                      in1=x_T[:, k * TS:(k + 1) * TS],
                                      op=Alu.add)
            inc(ins, "v")
        V(f_dve)
        new_phase()

        emit_ln(y_T, ln2_T)

        sq_prod2 = []
        te_rms0 = cnt["te"]

        def f_sca(e):
            for k in range(DT):
                p = k % 2
                if k < 2:
                    need(e, "s", "sqfree%d" % p)
                else:
                    w(e, "s", "te", te_rms0 + (k - 2) + 1)
                ins = e.activation(out=(sg0 if p == 0 else sg1)[:],
                                   in_=ln2_T[:, k * TS:(k + 1) * TS],
                                   func=Act.Square)
                sq_prod2.append(inc(ins, "s"))
        S(f_sca)

        def f_te(e):
            need(e, "te", "ps7")
            for k in range(DT):
                p = k % 2
                w(e, "te", "s", sq_prod2[k])
                ins = e.matmul(ps[7][0:1, 0:TS], onec[:],
                               (sg0 if p == 0 else sg1)[:],
                               start=(k == 0), stop=(k == DT - 1))
                inc(ins, "te")
                done("sqfree%d" % p, "te")
        TE(f_te)
        te_sq = cnt["te"]

        def f_dve2(e):
            w(e, "v", "te", te_sq)
            e.tensor_scalar(out=stat[0:1, 0:TS], in0=ps[7][0:1, 0:TS],
                            scalar1=1.0 / D, scalar2=RMS_EPS,
                            op0=Alu.mult, op1=Alu.add)
            inc(e.reciprocal(out=stat[0:1, 0:TS], in_=stat[0:1, 0:TS]), "v")
            done("ps7", "v")
        V(f_dve2)
        v_r2 = cnt["v"]

        def f_sca2(e):
            w(e, "s", "v", v_r2)
            inc(e.activation(out=stat[0:1, 0:TS], in_=stat[0:1, 0:TS],
                             func=Act.Sqrt), "s")
        S(f_sca2)
        s_r2 = cnt["s"]

        def f_te2(e):
            w(e, "te", "s", s_r2)
            need(e, "te", "ps2")
            inc(e.matmul(ps[2][:, 0:TS], crow[0:1, CO_ONES:CO_ONES + 128],
                         stat[0:1, 0:TS], start=True, stop=True), "te")
        TE(f_te2)
        te_r2 = cnt["te"]

        def f_dve3(e):
            w(e, "v", "te", te_r2)
            for k in range(DT):
                ins = e.scalar_tensor_tensor(
                    out=h2_T[:, k * TS:(k + 1) * TS],
                    in0=ln2_T[:, k * TS:(k + 1) * TS],
                    scalar=nw_s[:, k:k + 1], in1=ps[2][:, 0:TS],
                    op0=Alu.mult, op1=Alu.mult)
            inc(ins, "v")
            done("ps2", "v")
        V(f_dve3)
        new_phase()

        def f_te3(e):
            need(e, "te", "ps6")
            for i in range(TT):
                for k in range(DT):
                    e.matmul(ps[6][:, i * NE:(i + 1) * NE],
                             h2_T[:, k * TS + i * 128:k * TS + (i + 1) * 128],
                             w_gw[:, k * NE:(k + 1) * NE],
                             start=(k == 0), stop=False)
                ins = e.matmul(ps[6][:, i * NE:(i + 1) * NE],
                               crow_bf[0:1, CO_ONES:CO_ONES + 128],
                               crow_bf[0:1, CO_GB:CO_GB + NE],
                               start=False, stop=True)
            inc(ins, "te")
        TE(f_te3)
        te_gl = cnt["te"]

        MO = 3 * TT

        def f_dve4(e):
            w(e, "v", "te", te_gl)
            vsync(e, e.tensor_copy(out=glg[:], in_=ps[6][:, 0:TT * NE]))
            g3 = glg[:].rearrange("p (i e) -> p i e", i=TT)
            m1 = gm[:, 0:TT]
            m2 = gm[:, TT:2 * TT]
            nm1 = gm[:, 2 * TT:3 * TT]
            mskf = gm[:, MO:MO + TT * NE]
            msk2f = gm[:, MO + TT * NE:MO + 2 * TT * NE]
            msk2 = msk2f.rearrange("p (i e) -> p i e", i=TT)
            for i in range(TT):
                ins = e.tensor_reduce(out=m1[:, i:i + 1],
                                      in_=glg[:, i * NE:(i + 1) * NE],
                                      axis=AxX, op=Alu.max)
            vsync(e, ins)
            for i in range(TT):
                ins = e.tensor_tensor(out=mskf[:, i * NE:(i + 1) * NE],
                                      in0=glg[:, i * NE:(i + 1) * NE],
                                      in1=m1[:, i:i + 1].broadcast_to([128, NE]),
                                      op=Alu.is_equal)
            vsync(e, ins)
            vsync(e, e.scalar_tensor_tensor(out=msk2f, in0=mskf, scalar=NEG_BIG,
                                            in1=glg[:], op0=Alu.mult,
                                            op1=Alu.add))
            for i in range(TT):
                ins = e.tensor_reduce(out=m2[:, i:i + 1],
                                      in_=msk2f[:, i * NE:(i + 1) * NE],
                                      axis=AxX, op=Alu.max)
            vsync(e, ins)
            for i in range(TT):
                ins = e.tensor_tensor(out=mskf[:, i * NE:(i + 1) * NE],
                                      in0=glg[:, i * NE:(i + 1) * NE],
                                      in1=m2[:, i:i + 1].broadcast_to([128, NE]),
                                      op=Alu.is_ge)
            vsync(e, ins)
            ins = e.tensor_scalar_mul(out=nm1, in0=m1, scalar1=-1.0)
            inc(ins, "v")
            done("ps6", "v")
        V(f_dve4)
        v_g1 = cnt["v"]

        def f_sca3(e):
            w(e, "s", "v", v_g1)
            nm1 = gm[:, 2 * TT:3 * TT]
            for i in range(TT):
                ins = e.activation(out=glg[:, i * NE:(i + 1) * NE],
                                   in_=glg[:, i * NE:(i + 1) * NE],
                                   func=Act.Exp, bias=nm1[:, i:i + 1])
            inc(ins, "s")
        S(f_sca3)
        s_g = cnt["s"]

        def f_dve5(e):
            w(e, "v", "s", s_g)
            g3 = glg[:].rearrange("p (i e) -> p i e", i=TT)
            mskf = gm[:, MO:MO + TT * NE]
            ssum = gm[:, 0:TT]
            vsync(e, e.tensor_tensor(out=glg[:], in0=glg[:], in1=mskf,
                                     op=Alu.mult))
            for i in range(TT):
                ins = e.tensor_reduce(out=ssum[:, i:i + 1],
                                      in_=glg[:, i * NE:(i + 1) * NE],
                                      axis=AxX, op=Alu.add)
            vsync(e, ins)
            vsync(e, e.reciprocal(out=ssum, in_=ssum))
            for i in range(TT):
                ins = e.tensor_tensor(
                    out=glg[:, i * NE:(i + 1) * NE],
                    in0=glg[:, i * NE:(i + 1) * NE],
                    in1=ssum[:, i:i + 1].broadcast_to([128, NE]), op=Alu.mult)
            inc(ins, "v")
        V(f_dve5)
        v_g2 = cnt["v"]

        def f_te4(e):
            w(e, "te", "v", v_g2)
            need(e, "te", "ps6")
            for i in range(TT):
                ins = e.transpose(ps[6][0:NE, i * 128:(i + 1) * 128],
                                  glg[:, i * NE:(i + 1) * NE], ident[:])
            inc(ins, "te")
        TE(f_te4)
        te_gt = cnt["te"]

        def f_dve6(e):
            w(e, "v", "te", te_gt)
            e.tensor_copy(out=gT[:], in_=ps[6][0:NE, 0:TS])
            inc(e.tensor_scalar_mul(out=gTd[:], in0=gT[:],
                                    scalar1=1.0 / ALPHA), "v")
            done("ps6", "v")
        V(f_dve6)
        new_phase()

        def f_dve7(e):
            for k in range(DT):
                ins = e.tensor_tensor(out=y2[:, k * TS:(k + 1) * TS],
                                      in0=y_T[:, k * TS:(k + 1) * TS],
                                      in1=ln2_T[:, k * TS:(k + 1) * TS],
                                      op=Alu.add)
            inc(ins, "v")
        V(f_dve7)

        b2_prod = []
        v_b20 = cnt["v"]

        def f_te5(e):
            for md in range(DT):
                bank = md % 2
                if md < 2:
                    need(e, "te", "ps%d" % bank)
                else:
                    w(e, "te", "v", v_b20 + (md - 2) + 1)
                ins = e.matmul(ps[bank][:, 0:TS],
                               b2_s[:, md * 128:(md + 1) * 128], gT[:],
                               start=True, stop=True)
                b2_prod.append(inc(ins, "te"))
        TE(f_te5)

        def f_dve8(e):
            for md in range(DT):
                bank = md % 2
                w(e, "v", "te", b2_prod[md])
                ins = e.scalar_tensor_tensor(
                    out=y2[:, md * TS:(md + 1) * TS], in0=ps[bank][:, 0:TS],
                    scalar=1.0, in1=y2[:, md * TS:(md + 1) * TS],
                    op0=Alu.mult, op1=Alu.add)
                inc(ins, "v")
                done("ps%d" % bank, "v")
        V(f_dve8)
        new_phase()

        for e_i in range(NE):
            if sup == 0 and e_i == 0:
                tap("t_glg", glg[:])
                tap("t_h2", h2_T[:])
                tap("t_y2i", y2[:])
            emit_expert(e_i, y2)
            if sup == 0 and e_i == 0:
                tap("t_act", actp[:])
                tap("t_y2e0", y2[:])
        if sup == 0:
            tap("t_y2f", y2[:])
        new_phase()

        ot_prod = []
        v_ot0 = cnt["v"]

        def f_te6(e):
            for k in range(DT):
                bank = 2 + (k % 2)
                if k < 2:
                    need(e, "te", "ps%d" % bank)
                else:
                    w(e, "te", "v", v_ot0 + (k - 2) + 1)
                for i in range(TT):
                    ins = e.transpose(
                        ps[bank][:, i * 128:(i + 1) * 128],
                        y2[:, k * TS + i * 128:k * TS + (i + 1) * 128],
                        ident[:])
                ot_prod.append(inc(ins, "te"))
        TE(f_te6)

        def f_dve9(e):
            for k in range(DT):
                bank = 2 + (k % 2)
                w(e, "v", "te", ot_prod[k])
                for i in range(TT):
                    ins = e.tensor_copy(
                        out=x_tp[:, i * D + k * 128:i * D + (k + 1) * 128],
                        in_=ps[bank][:, i * 128:(i + 1) * 128])
                inc(ins, "v")
                done("ps%d" % bank, "v")
        V(f_dve9)
        v_out = cnt["v"]

        def f_sy(e):
            w(e, "sy", "v", v_out)
            ins = e.dma_start(
                out=dy[t0:t0 + TS, :].rearrange("(i p) d -> p i d", p=128),
                in_=x_tp[:].rearrange("p (i d) -> p i d", i=TT))
            inc(ins, "do", 16)
        SY(f_sy)
        new_phase()

    def emit_expert(e_i, y2):
        w1_prod = []
        w2_prod = [0]
        b1_prod = [0]

        def f_gp(e, e_i=e_i):
            need(e, "g", "ring1")
            for h in range(4):
                ins = e.dma_start(
                    out=ring1[:, h * 2 * I2:(h + 1) * 2 * I2]
                    .rearrange("p (q c) -> p q c", q=2),
                    in_=d_w1[e_i][h * 256:(h + 1) * 256, :]
                    .rearrange("(q p) c -> p q c", p=128))
                w1_prod.append(inc(ins, "dg", 16))
            need(e, "g", "ring2")
            for h in range(4):
                ins = e.dma_start(
                    out=ring2[:, h * 2 * D:(h + 1) * 2 * D]
                    .rearrange("p (q c) -> p q c", q=2),
                    in_=d_w2[e_i][h * 256:(h + 1) * 256, :]
                    .rearrange("(q p) c -> p q c", p=128))
                w2_prod[0] = inc(ins, "dg", 16)
            need(e, "g", "b1row")
            b1_prod[0] = inc(
                e.dma_start(out=crow_bf[0:1, CO_B1:CO_B1 + I2],
                            in_=d_b1[e_i:e_i + 1, :]), "dg", 16)
        G(f_gp)

        gr_prod = [0]

        def f_te0(e, e_i=e_i):
            need(e, "te", "ps4")
            gr_prod[0] = inc(
                e.matmul(ps[4][:, 0:TS], sel[0:NE, e_i * 128:(e_i + 1) * 128],
                         gTd[0:NE, :], start=True, stop=True), "te")
        TE(f_te0)

        m1_prod = {}
        for pr in range(PAIR):
            def f_te(e, pr=pr):
                need(e, "te", "ps0")
                need(e, "te", "ps1")
                w(e, "te", "dg", b1_prod[0])
                for m2i, bank in ((pr, 0), (PAIR + pr, 1)):
                    for k in range(DT):
                        e.matmul(ps[bank][:, 0:TS],
                                 ring1[:, k * I2 + m2i * 128:
                                       k * I2 + (m2i + 1) * 128],
                                 h2_T[:, k * TS:(k + 1) * TS],
                                 start=(k == 0), stop=False)
                    ins = e.matmul(
                        ps[bank][:, 0:TS],
                        crow_bf[0:1, CO_B1 + m2i * 128:CO_B1 + (m2i + 1) * 128],
                        crow_bf[0:1, CO_ONES:CO_ONES + TS],
                        start=False, stop=True)
                    m1_prod[(pr, bank)] = inc(ins, "te")
            TE(f_te)

            vglu = [0]

            def f_dve(e, pr=pr):
                w(e, "v", "te", m1_prod[(pr, 0)])
                need(e, "v", "sg0w")
                vglu[0] = inc(e.tensor_scalar_min(out=sg0[:],
                                                  in0=ps[0][:, 0:TS],
                                                  scalar1=LIMIT), "v")
            V(f_dve)

            ssil = [0]

            def f_sca(e, vglu=vglu):
                w(e, "s", "v", vglu[0])
                need(e, "s", "sg1w")
                ssil[0] = inc(e.activation(out=sg1[:], in_=sg0[:],
                                           func=Act.Silu, scale=ALPHA), "s")
            S(f_sca)

            def f_dve2(e, pr=pr, ssil=ssil):
                w(e, "v", "te", m1_prod[(pr, 1)])
                e.tensor_scalar(out=ab1[:], in0=ps[1][:, 0:TS], scalar1=LIMIT,
                                scalar2=-LIMIT, op0=Alu.min, op1=Alu.max)
                done("ps0", "v")
                done("ps1", "v")
                w(e, "v", "s", ssil[0])
                e.scalar_tensor_tensor(out=sg0[:], in0=ab1[:], scalar=1.0,
                                       in1=sg1[:], op0=Alu.add, op1=Alu.mult)
                done("sg1w", "v")
                w(e, "v", "te", gr_prod[0])
                ins = e.tensor_tensor(out=actp[:, pr * TS:(pr + 1) * TS],
                                      in0=sg0[:], in1=ps[4][:, 0:TS],
                                      op=Alu.mult)
                inc(ins, "v")
                done("sg0w", "v")
            V(f_dve2)
        done("ps4", "v")
        v_act = cnt["v"]

        m2_prod = []
        v_m20 = cnt["v"]

        def f_te2(e, e_i=e_i):
            w(e, "te", "v", v_act)
            w(e, "te", "dg", b1_prod[0])
            for md in range(DT):
                bank = md % 2
                if md < 2:
                    need(e, "te", "ps%d" % bank)
                else:
                    w(e, "te", "v", v_m20 + (md - 2) + 1)
                for k in range(IIT):
                    ins = e.matmul(ps[bank][:, 0:TS],
                                   ring2[:, k * D + md * 128:
                                         k * D + (md + 1) * 128],
                                   actp[:, k * TS:(k + 1) * TS],
                                   start=(k == 0), stop=(k == IIT - 1))
                m2_prod.append(inc(ins, "te"))
        TE(f_te2)

        def f_dve3(e):
            for md in range(DT):
                bank = md % 2
                w(e, "v", "te", m2_prod[md])
                ins = e.scalar_tensor_tensor(
                    out=y2[:, md * TS:(md + 1) * TS], in0=ps[bank][:, 0:TS],
                    scalar=1.0, in1=y2[:, md * TS:(md + 1) * TS],
                    op0=Alu.mult, op1=Alu.add)
                inc(ins, "v")
                done("ps%d" % bank, "v")
        V(f_dve3)
        done("ring1", "v")
        done("ring2", "v")
        done("b1row", "te")

    for sup in range(NSUP):
        emit_load_transpose(sup, prefix=True)
        emit_ln(x_T, ln_T)
        emit_dt_u(sup, prefix=True)
        emit_bc_proj(sup, prefix=True)
        emit_uy(sup, prefix=True)
        emit_scan_prefix(sup)
    for sup in range(NSUP):
        emit_load_transpose(sup, prefix=False)
        if sup == 0:
            tap("t_xT", x_T[:])
        emit_ln(x_T, ln_T)
        if sup == 0:
            tap("t_lnT", ln_T[:])
            tap("t_stat", stat[:])
        emit_dt_u(sup, prefix=False)
        if sup == 0:
            tap("t_dtT", dt_T[:])
            tap("t_uT", u_T[:])
        emit_bc_proj(sup, prefix=False)
        emit_uy(sup, prefix=False)
        emit_scan_main(sup)
        if sup == 0:
            tap("t_yscan", y_T[:])
            tap("t_hst", hst[:])
        emit_post_ssm(sup)

    def f_fin(e):
        w(e, "sy", "do", cnt["do"])
    SY(f_fin)

    es.close()
    return nc


def _prep(inputs):
    perm = np.concatenate([np.arange(0, I2, 2), np.arange(1, I2, 2)])
    w = {}
    w["dtwT"] = np.ascontiguousarray(inputs["L0_attn_dt_w"].T)
    w["bwT"] = np.ascontiguousarray(inputs["L0_attn_B_w"].T)
    w["cwT"] = np.ascontiguousarray(inputs["L0_attn_C_w"].T)
    w["gwT"] = np.ascontiguousarray(inputs["L0_ffn_gate_w"].T)
    w["logA"] = np.ascontiguousarray(inputs["L0_attn_log_A"])
    w["dtb"] = np.ascontiguousarray(inputs["L0_attn_dt_b"].reshape(1, D))
    w["bb"] = np.ascontiguousarray(inputs["L0_attn_B_b"].reshape(1, NST))
    w["cb"] = np.ascontiguousarray(inputs["L0_attn_C_b"].reshape(1, NST))
    w["gb"] = np.ascontiguousarray(inputs["L0_ffn_gate_b"].reshape(1, NE))
    w["dsk"] = np.ascontiguousarray(inputs["L0_attn_D_skip"].reshape(D, 1))
    w["nw"] = np.ascontiguousarray(inputs["L0_ffn_norm_w"].reshape(D, 1))
    w["w1T"] = np.ascontiguousarray(
        inputs["L0_ffn_mlp1_w"].transpose(0, 2, 1)[:, :, perm])
    w["b1g"] = np.ascontiguousarray(inputs["L0_ffn_mlp1_b"][:, perm])
    w["w2T"] = np.ascontiguousarray(inputs["L0_ffn_mlp2_w"].transpose(0, 2, 1))
    w["b2"] = np.ascontiguousarray(inputs["L0_ffn_mlp2_b"])
    for k in w:
        w[k] = np.asarray(w[k], dtype=np.float32)
    return w


def kernel(**inputs):
    x = np.ascontiguousarray(np.asarray(inputs["L0_attn_x"], dtype=np.float32))
    w = _prep(inputs)
    zeros = np.zeros((TOK, D), dtype=np.float32)
    in_maps = []
    for c in range(8):
        b, half = c // 2, c % 2
        m = dict(w)
        m["x_own"] = np.ascontiguousarray(x[b, half * TOK:(half + 1) * TOK, :])
        m["x_pre"] = np.ascontiguousarray(x[b, 0:TOK, :]) if half == 1 else zeros
        in_maps.append(m)
    nc = build_nc()
    res = run_bass_kernel_spmd(nc, in_maps, core_ids=list(range(8)))
    out = np.empty((B, T, D), dtype=np.float32)
    for c in range(8):
        b, half = c // 2, c % 2
        out[b, half * TOK:(half + 1) * TOK, :] = np.asarray(res.results[c]["y"])
    return out
